# version 05
# baseline (speedup 1.0000x reference)
import numpy as np
import ml_dtypes
import contextlib
import concourse.bass as bass
import concourse.mybir as mybir
from concourse.bass_utils import run_bass_kernel_spmd

F32 = mybir.dt.float32
BF16 = mybir.dt.bfloat16
AF = mybir.ActivationFunctionType
ALU = mybir.AluOpType

D = 4096
T = 2048
NCD = D // 128
NH = 32
NKV = 8
EPS = 1e-5
SCALE = 1.0 / float(np.sqrt(128.0))
N_SUB = 4


class Sched:
    ENGINES = ("sync", "tensor", "vector", "scalar", "gpsimd")

    def __init__(self):
        self.ops = {e: [] for e in self.ENGINES}
        self.cnt = {}

    def emit(self, eng, fn, waits=(), inc=None, amt=1, self_inc=False):
        ws = tuple((s, v) for (s, v) in waits if v > 0)
        if inc is not None:
            self.cnt[inc] = self.cnt.get(inc, 0) + amt
        self.ops[eng].append((fn, ws, inc, amt, self_inc))
        return self.cnt.get(inc)

    def now(self, s):
        return self.cnt.get(s, 0)


def pieces_of(off, width=1024):
    out = []
    while off < width:
        end = min(width, (off // 512 + 1) * 512)
        out.append((off, end - off))
        off = end
    return out


def build_program(n_sub=N_SUB):
    nc = bass.Bass("TRN2", target_bir_lowering=False)
    dr = lambda name, shape, dt, kind: nc.dram_tensor(name, shape, dt, kind=kind).ap()
    xT = dr("xT", [D, T], F32, "ExternalInput")
    wqkv = dr("wqkv", [D, 6144], F32, "ExternalInput")
    wo_a = dr("wo_a", [D, D], F32, "ExternalInput")
    wq_b = dr("wq_b", [D, D], F32, "ExternalInput")
    wk_b = dr("wk_b", [D, 1024], F32, "ExternalInput")
    wv_b = dr("wv_b", [D, 1024], F32, "ExternalInput")
    wo_b = dr("wo_b", [D, D], F32, "ExternalInput")
    bwT = dr("bwT", [D, D], F32, "ExternalInput")
    bb1 = dr("bb1", [1, D], F32, "ExternalInput")
    gwT = dr("gwT", [D, 8], F32, "ExternalInput")
    aT = dr("aT", [D, 128], F32, "ExternalInput")
    bT = dr("bT", [128, D], F32, "ExternalInput")
    cdupd = dr("cdup", [128, T], F32, "ExternalInput")
    sdupd = dr("sdup", [128, T], F32, "ExternalInput")
    trid = dr("tri", [128, 128], F32, "ExternalInput")
    maskbd = dr("maskb", [128, 128], F32, "ExternalInput")
    identd = dr("ident", [128, 128], F32, "ExternalInput")
    e8d = dr("e8", [8, 128], F32, "ExternalInput")
    outT = dr("outT", [D, T], F32, "ExternalOutput")
    sA = dr("sA", [D, T], F32, "Internal")
    sB = dr("sB", [D, T], F32, "Internal")
    xbA = dr("xbA", [D, T], BF16, "Internal")
    xbB = dr("xbB", [D, T], BF16, "Internal")
    qS = dr("qS", [NH, 128, T], BF16, "Internal")
    kS = dr("kS", [NKV, 128, T], BF16, "Internal")
    vS = dr("vS", [NKV, T, 128], BF16, "Internal")
    oS = dr("oS", [NH, 128, T], BF16, "Internal")

    S = Sched()
    es = contextlib.ExitStack()
    with es:
        sb = lambda name, shape, dt: es.enter_context(nc.sbuf_tensor(name, shape, dt))
        xb = sb("xb", [128, NCD * T], BF16)
        arena = sb("arena", [128, 36864], BF16)
        tri = sb("tri_s", [128, 128], BF16)
        maskb = sb("maskb_s", [128, 128], F32)
        ident = sb("ident_s", [128, 128], F32)
        ones_cb = sb("ones_cb", [128, 1], BF16)
        ones_r32 = sb("ones_r32", [1, 128], F32)
        ones_rb = sb("ones_rb", [1, 128], BF16)
        e8 = sb("e8_s", [8, 128], F32)
        bbcols = sb("bbcols", [128, 32], F32)
        r_col = sb("r_col", [128, 16], F32)
        ps = es.enter_context(nc.psum_tensor("ps", [128, 4096], F32))
        sems = {n: es.enter_context(nc.semaphore(n))
                for n in ("gp", "sy", "pe", "act", "dve")}

        ar_f = arena.ap().bitcast(F32)

        def a_bf(off, cols):
            assert off % 2 == 0 and off // 2 + cols <= 36864, (off, cols)
            return arena[:, off // 2: off // 2 + cols]

        def a_f32(off, cols):
            assert off % 4 == 0 and off // 4 + cols <= 18432, (off, cols)
            return ar_f[:, off // 4: off // 4 + cols]

        sx_row = a_f32(53248, 2048)[0:1, :]
        msq_row = sx_row
        mbf_row = a_bf(53248, 2048)[0:1, :]
        sxx_row = a_f32(61440, 2048)[0:1, :]
        rstd_row = sxx_row
        m_row = a_f32(20480, 2048)[0:1, :]
        var_row = a_f32(28672, 2048)[0:1, :]
        rs_row = a_f32(53248, 1024)[0:1, :]
        rr_row = a_f32(57344, 1024)[0:1, :]
        bbrow = a_f32(53248, 4096)[0:1, :]

        S.emit("gpsimd", lambda g: g.dma_start(out=tri[:], in_=trid), inc="gp", amt=16)
        S.emit("gpsimd", lambda g: g.dma_start(out=maskb[:], in_=maskbd), inc="gp", amt=16)
        S.emit("sync", lambda s: s.dma_start(out=ident[:], in_=identd), inc="sy", amt=16)
        S.emit("sync", lambda s: s.dma_start(out=e8[:], in_=e8d), inc="sy", amt=16)
        S.emit("gpsimd", lambda g: g.memset(ones_cb[:], 1.0), inc="gp", amt=1)
        S.emit("gpsimd", lambda g: g.memset(ones_r32[:], 1.0), inc="gp", amt=1)
        S.emit("gpsimd", lambda g: g.memset(ones_rb[:], 1.0), inc="gp", amt=1)
        S.emit("sync", lambda s: s.dma_start(out=bbrow, in_=bb1),
               inc="sy", amt=16)
        setup_sy = S.now("sy")
        setup_gp = S.now("gp")

        def bb_tr(t):
            last = None
            for c in range(32):
                last = t.transpose(ps[:, 3584 + c: 3585 + c],
                                   bbrow[:, c * 128:(c + 1) * 128],
                                   ident[0:1, 0:1])
            return last
        bbp = S.emit("tensor", bb_tr, waits=[("sy", setup_sy), ("gp", setup_gp)],
                     inc="pe")
        bbd = S.emit("vector", lambda v: v.tensor_copy(bbcols[:], ps[:, 3584:3616]),
                     waits=[("pe", bbp)], inc="dve")
        st = {"bb_done": bbd}

        def emit_stats(src, want_rcol, src_sy, src_bf=None):
            xsq = [a_bf(0, 2048), a_bf(4096, 2048)]
            M128 = a_bf(8192, 2048)
            R128 = a_f32(12288, 2048)
            xb_free = st.get("xb_free", 0)
            ar_free = st.get("ar_free_stats", 0)
            ld = {}
            if src_bf is None:
                for c in range(NCD):
                    ld[c] = S.emit("gpsimd", lambda g, c=c: g.dma_start(
                        out=xb[:, c * T:(c + 1) * T],
                        in_=src[c * 128:(c + 1) * 128, :]),
                        waits=[("pe", xb_free), ("sy", src_sy)],
                        inc="gp", amt=16)
            else:
                for grp in range(NCD // 4):
                    cc = S.emit("sync", lambda s, grp=grp: s.dma_start(
                        out=xb[:, grp * 4 * T:(grp + 1) * 4 * T].rearrange(
                            "p (c t) -> p c t", t=T),
                        in_=src_bf[grp * 512:(grp + 1) * 512, :].rearrange(
                            "(c p) t -> p c t", p=128)),
                        waits=[("pe", xb_free), ("sy", src_sy)],
                        inc="sy", amt=16)
                    for c in range(grp * 4, grp * 4 + 4):
                        ld[c] = ("sy", cc)
            def _ldw(c):
                v = ld[c]
                return v if isinstance(v, tuple) else ("gp", v)
            sq = {}
            mmc = {}
            for c in range(NCD):
                sqw = [_ldw(c), ("dve", ar_free)]
                if c >= 2:
                    sqw.append(("pe", mmc[c - 2]))
                sq[c] = S.emit("scalar", lambda s, c=c: s.activation(
                    xsq[c % 2][:], xb[:, c * T:(c + 1) * T], AF.Square),
                    waits=sqw, inc="act")

                def statmm(t, c=c):
                    last = None
                    for tc in range(4):
                        t.matmul(ps[0:1, tc * 512:(tc + 1) * 512], ones_cb[:],
                                 xb[:, c * T + tc * 512: c * T + (tc + 1) * 512],
                                 start=(c == 0), stop=(c == NCD - 1))
                        last = t.matmul(
                            ps[0:1, 2048 + tc * 512: 2048 + (tc + 1) * 512],
                            ones_cb[:], xsq[c % 2][:, tc * 512:(tc + 1) * 512],
                            start=(c == 0), stop=(c == NCD - 1))
                    return last
                mmc[c] = S.emit("tensor", statmm,
                                waits=[_ldw(c), ("gp", setup_gp),
                                       ("act", sq[c])], inc="pe")
            stats_done = mmc[NCD - 1]
            S.emit("vector", lambda v: v.tensor_copy(sx_row[:], ps[0:1, 0:2048]),
                   waits=[("pe", stats_done)], inc="dve")
            S.emit("vector", lambda v: v.tensor_copy(sxx_row[:], ps[0:1, 2048:4096]),
                   inc="dve")
            S.emit("vector", lambda v: v.tensor_scalar_mul(m_row[:], sx_row[:],
                                                           1.0 / D), inc="dve")
            S.emit("vector", lambda v: v.tensor_tensor(msq_row[:], m_row[:],
                                                       m_row[:], ALU.mult),
                   inc="dve")
            S.emit("vector", lambda v: v.tensor_scalar_mul(var_row[:], sxx_row[:],
                                                           1.0 / D), inc="dve")
            S.emit("vector", lambda v: v.tensor_sub(var_row[:], var_row[:],
                                                    msq_row[:]), inc="dve")
            vv = S.emit("vector", lambda v: v.tensor_scalar_add(
                var_row[:], var_row[:], EPS), inc="dve")
            S.emit("scalar", lambda s: s.activation(rstd_row[:], var_row[:],
                                                    AF.Ln),
                   waits=[("dve", vv)], inc="act")
            l2 = S.emit("scalar", lambda s: s.activation(rstd_row[:], rstd_row[:],
                                                         AF.Exp, scale=-0.5),
                        inc="act")
            mb = S.emit("vector", lambda v: v.tensor_copy(mbf_row[:], m_row[:]),
                        waits=[("act", l2)], inc="dve")

            def bcasts(t):
                last = None
                for tc in range(4):
                    t.matmul(ps[:, tc * 512:(tc + 1) * 512], ones_rb[:],
                             mbf_row[0:1, tc * 512:(tc + 1) * 512],
                             start=True, stop=True)
                    last = t.matmul(ps[:, 2048 + tc * 512: 2048 + (tc + 1) * 512],
                                    ones_r32[:],
                                    rstd_row[0:1, tc * 512:(tc + 1) * 512],
                                    start=True, stop=True)
                return last
            bc = S.emit("tensor", bcasts, waits=[("dve", mb)], inc="pe")
            e1 = S.emit("vector", lambda v: v.tensor_copy(M128[:], ps[:, 0:2048]),
                        waits=[("pe", bc)], inc="dve")
            e2 = S.emit("vector", lambda v: v.tensor_copy(R128[:], ps[:, 2048:4096]),
                        inc="dve")
            rc = e2
            if want_rcol:
                def rctr(t):
                    last = None
                    for i in range(16):
                        last = t.transpose(ps[:, 3584 + i:3585 + i],
                                           rstd_row[0:1, i * 128:(i + 1) * 128],
                                           ident[0:1, 0:1])
                    return last
                rt = S.emit("tensor", rctr, waits=[("dve", e2)], inc="pe")
                rc = S.emit("vector", lambda v: v.tensor_copy(
                    r_col[:], ps[:, 3584:3600]), waits=[("pe", rt)], inc="dve")
            xcd = 0
            for c in range(NCD):
                xcd = S.emit("vector", lambda v, c=c: v.tensor_sub(
                    xb[:, c * T:(c + 1) * T], xb[:, c * T:(c + 1) * T], M128[:]),
                    waits=[("dve", rc), ("pe", stats_done)], inc="dve")
            return dict(xc_done=xcd, R128=R128, rcol_done=rc, R128_evac=e2,
                        stats_pe=stats_done)

        def emit_attn(src, dst, qk_parts, v_part, wo, src_bf, dst_bf):
            src_sy = st.get("src_ready_sy", 0)
            stt = emit_stats(src, True, src_sy, src_bf)
            R128 = stt["R128"]
            cR2 = a_bf(36864, 2048)
            sR2 = a_bf(40960, 2048)
            cd = a_f32(45056, 2048)
            sd = a_f32(53248, 2048)
            S.emit("sync", lambda s: s.dma_start(out=cd, in_=cdupd),
                   waits=[("dve", st.get("ar_rope_free", 0)),
                          ("dve", stt["R128_evac"])], inc="sy", amt=16)
            tl2 = S.emit("sync", lambda s: s.dma_start(out=sd, in_=sdupd),
                         waits=[("dve", stt["R128_evac"])], inc="sy", amt=16)
            tb1 = S.emit("vector", lambda v: v.tensor_tensor(cR2[:], cd, R128[:],
                                                             ALU.mult),
                         waits=[("sy", tl2), ("dve", stt["R128_evac"])],
                         inc="dve")
            tables_done = S.emit("vector", lambda v: v.tensor_tensor(
                sR2[:], sd, R128[:], ALU.mult), inc="dve")

            wsl = [a_bf(0, 8192), a_bf(16384, 8192)]
            u1 = a_bf(61440, 1024)
            u2 = a_bf(63488, 1024)
            stg = [a_bf(65536, 1024), a_bf(67584, 1024),
                   a_bf(69632, 1024), a_bf(71680, 1024)]
            pair_list = []
            for (w_ap, coff, npair, scr, hb) in qk_parts:
                for p in range(npair):
                    pair_list.append((w_ap, coff + p * 256, scr, hb + 2 * p))
            mm_done = {}
            rope_done = {}
            stage_free = {}
            slab_ready = {}
            for ui, (w_ap, coff, scr, h0) in enumerate(pair_list):
                src3 = w_ap[:, coff:coff + 256].rearrange("(c p) f -> p c f", p=128)
                ws = [("dve", tables_done)] if ui < 2 else \
                     [("pe", mm_done[ui - 2])]
                slab_ready[ui] = S.emit("gpsimd", lambda g, ui=ui, src3=src3:
                    g.dma_start(out=wsl[ui % 2].rearrange("p (c f) -> p c f",
                                                          f=256), in_=src3),
                    waits=ws, inc="gp", amt=16)
                for th in range(2):
                    un = ui * 2 + th
                    base = (un % 2) * 2048
                    pE, pO = base, base + 1024

                    def mms(t, ui=ui, th=th, pE=pE, pO=pO):
                        last = None
                        sl = wsl[ui % 2]
                        for d in range(NCD):
                            for (po, ofs) in ((pE, 0), (pO, 128)):
                                for tc in range(2):
                                    last = t.matmul(
                                        ps[:, po + tc * 512: po + (tc + 1) * 512],
                                        sl[:, d * 256 + ofs: d * 256 + ofs + 128],
                                        xb[:, d * T + th * 1024 + tc * 512:
                                           d * T + th * 1024 + (tc + 1) * 512],
                                        start=(d == 0), stop=(d == NCD - 1))
                        return last
                    mmu = S.emit("tensor", mms,
                                 waits=[("gp", slab_ready[ui]),
                                        ("dve", stt["xc_done"]),
                                        ("dve", rope_done.get(un - 2, 0))],
                                 inc="pe")
                    if th == 1:
                        mm_done[ui] = mmu
                    sg_t = stg[(un % 2) * 2]
                    sg_b = stg[(un % 2) * 2 + 1]
                    cs = cR2[:, th * 1024:(th + 1) * 1024]
                    ss = sR2[:, th * 1024:(th + 1) * 1024]
                    S.emit("vector", lambda v, cs=cs, pE=pE: v.tensor_tensor(
                        u1[:], cs, ps[:, pE:pE + 1024], ALU.mult),
                        waits=[("pe", mmu), ("sy", stage_free.get(un - 2, 0))],
                        inc="dve")
                    S.emit("vector", lambda v, ss=ss, pO=pO: v.tensor_tensor(
                        u2[:], ss, ps[:, pO:pO + 1024], ALU.mult), inc="dve")
                    S.emit("vector", lambda v, sg_t=sg_t: v.tensor_sub(
                        sg_t[:], u1[:], u2[:]), inc="dve")
                    S.emit("vector", lambda v, ss=ss, pE=pE: v.tensor_tensor(
                        u1[:], ss, ps[:, pE:pE + 1024], ALU.mult), inc="dve")
                    S.emit("vector", lambda v, cs=cs, pO=pO: v.tensor_tensor(
                        u2[:], cs, ps[:, pO:pO + 1024], ALU.mult), inc="dve")
                    rope_done[un] = S.emit("vector", lambda v, sg_b=sg_b:
                        v.tensor_add(sg_b[:], u1[:], u2[:]), inc="dve")

                    def store(s, scr=scr, h0=h0, th=th, sg_t=sg_t, sg_b=sg_b):
                        tsl = slice(th * 1024, (th + 1) * 1024)
                        sy = sems["sy"]
                        s.dma_start(out=scr[h0, 0:64, tsl],
                                    in_=sg_t[0:64, :]).then_inc(sy, 16)
                        s.dma_start(out=scr[h0 + 1, 0:64, tsl],
                                    in_=sg_t[64:128, :]).then_inc(sy, 16)
                        s.dma_start(out=scr[h0, 64:128, tsl],
                                    in_=sg_b[0:64, :]).then_inc(sy, 16)
                        s.dma_start(out=scr[h0 + 1, 64:128, tsl],
                                    in_=sg_b[64:128, :]).then_inc(sy, 16)
                    stage_free[un] = S.emit("sync", store,
                                            waits=[("dve", rope_done[un])],
                                            inc="sy", amt=64, self_inc=True)
            last_un = len(pair_list) * 2 - 1
            qk_stores = S.now("sy")
            qk_pe = S.now("pe")

            vw_ap, vcoff = v_part
            vsl = a_bf(0, 16384)
            vstg = [a_bf(61440, 512), a_bf(63488, 512)]
            v_mm = {}
            v_evac = {}
            v_st = {}
            for fh in range(2):
                src3 = vw_ap[:, vcoff + fh * 512: vcoff + (fh + 1) * 512] \
                    .rearrange("(c p) f -> p c f", p=128)
                vw0 = [("pe", qk_pe), ("dve", rope_done[last_un])] if fh == 0 \
                    else [("pe", v_mm[(0, 15)])]
                vsl_ld = S.emit("gpsimd", lambda g, src3=src3: g.dma_start(
                    out=vsl.rearrange("p (c f) -> p c f", f=512), in_=src3),
                    waits=vw0, inc="gp", amt=16)
                for tt in range(16):
                    pb = (tt % 2) * 512

                    def vmm(t, tt=tt, pb=pb):
                        last = None
                        for d in range(NCD):
                            last = t.matmul(
                                ps[:, pb:pb + 512],
                                xb[:, d * T + tt * 128: d * T + (tt + 1) * 128],
                                vsl[:, d * 512:(d + 1) * 512],
                                start=(d == 0), stop=(d == NCD - 1))
                        return last
                    vw = [("gp", vsl_ld), ("dve", stt["xc_done"])]
                    if (fh, tt - 2) in v_evac:
                        vw.append(("dve", v_evac[(fh, tt - 2)]))
                    elif fh > 0 and tt < 2:
                        vw.append(("dve", v_evac[(fh - 1, 14 + tt)]))
                    v_mm[(fh, tt)] = S.emit("tensor", vmm, waits=vw, inc="pe")
                    ev = S.emit("vector", lambda v, tt=tt, pb=pb:
                        v.tensor_scalar(vstg[tt % 2][:], ps[:, pb:pb + 512],
                                        r_col[:, tt:tt + 1], None, ALU.mult),
                        waits=[("pe", v_mm[(fh, tt)]), ("dve", stt["rcol_done"]),
                               ("sy", max(qk_stores, v_st.get((fh, tt - 2), 0)))],
                        inc="dve")
                    v_evac[(fh, tt)] = ev
                    v_st[(fh, tt)] = S.emit("sync", lambda s, fh=fh, tt=tt:
                        s.dma_start(
                            out=vS[fh * 4:(fh + 1) * 4,
                                   tt * 128:(tt + 1) * 128, :].rearrange(
                                       "h p n -> p h n"),
                            in_=vstg[tt % 2].rearrange("p (h n) -> p h n",
                                                       n=128)),
                        waits=[("dve", ev)], inc="sy", amt=16)
            v_pe_done = S.now("pe")
            v_stores = S.now("sy")

            kbuf = [a_bf(0, 2048), a_bf(4096, 2048)]
            vbuf = [a_bf(8192, 2048), a_bf(12288, 2048)]
            qbuf = [a_bf(16384, 1024), a_bf(18432, 1024), a_bf(51200, 1024)]
            diag = a_bf(20480, 8192)
            rot = [a_bf(36864, 1024), a_bf(38912, 1024), a_bf(40960, 1024)]
            rb_sb = a_f32(43008, 1024)
            ostg = [a_bf(47104, 1024), a_bf(49152, 1024)]
            o_free = S.now("dve")
            p3_dve0 = S.now("dve")
            units = [(h, qc) for h in range(NH) for qc in range(2)]
            prev_exp = {}
            sidx_ctr = [0]
            pending = [None]

            def flush_pending():
                if pending[0] is not None:
                    fn = pending[0]
                    pending[0] = None
                    fn()
            kv_ld = {}
            kv_last = {}
            q_ld = {}
            q_free = {}
            ost_free = {}

            def kvload(s, g, kb, vb):
                sy = sems["sy"]
                s.dma_start(out=kb[:], in_=kS[g]).then_inc(sy, 16)
                s.dma_start(out=vb.rearrange("p (k n) -> p k n", n=128),
                            in_=vS[g].rearrange("(k p) n -> p k n",
                                                p=128)).then_inc(sy, 16)
            for ui, (h, qc) in enumerate(units):
                g = h // 4
                if ui == 0:
                    kv_ld[0] = S.emit("sync", lambda s: kvload(
                        s, 0, kbuf[0], vbuf[0]),
                        waits=[("sy", max(qk_stores, v_stores)),
                               ("pe", v_pe_done),
                               ("dve", rope_done[last_un])], inc="sy", amt=32,
                        self_inc=True)
                    kv_ld[1] = S.emit("sync", lambda s: kvload(
                        s, 1, kbuf[1], vbuf[1]), inc="sy", amt=32,
                        self_inc=True)
                if qc == 0 and h % 4 == 0 and g >= 1 and g + 1 < NKV:
                    kv_ld[g + 1] = S.emit("sync", lambda s, g=g: kvload(
                        s, g + 1, kbuf[(g + 1) % 2], vbuf[(g + 1) % 2]),
                        waits=[("pe", kv_last[g - 1])], inc="sy", amt=32,
                        self_inc=True)
                q_ld[ui] = S.emit("sync", lambda s, h=h, qc=qc, ui=ui:
                    s.dma_start(out=qbuf[ui % 3][:],
                                in_=qS[h][:, qc * 1024:(qc + 1) * 1024]),
                    waits=[("sy", qk_stores), ("pe", q_free.get(ui - 3, 0)),
                           ("pe", v_pe_done)], inc="sy", amt=16)

                kb, vb = kbuf[g % 2], vbuf[g % 2]
                qb = qbuf[ui % 3]
                nfull = 8 * qc
                exp_done = {}
                n_rspv = 0

                def s_exp(kt, off, dst, diag_mask=False):
                    sidx = sidx_ctr[0]
                    sidx_ctr[0] += 1
                    soff = [0, 1024][sidx % 2]
                    pcs = pieces_of(off)

                    def smm(t, kt=kt, soff=soff, pcs=pcs, kb=kb, qb=qb):
                        last = None
                        for (o, w) in pcs:
                            last = t.matmul(ps[:, soff + o: soff + o + w],
                                            kb[:, kt * 128:(kt + 1) * 128],
                                            qb[:, o:o + w], start=True,
                                            stop=True)
                        return last
                    ws = [("act", prev_exp.get(sidx % 2, 0)),
                          ("sy", max(q_ld[ui], kv_ld[g]))]
                    if sidx < 2:
                        ws.append(("dve", p3_dve0))
                    pc = S.emit("tensor", smm, waits=ws, inc="pe")

                    def ef(s, soff=soff, off=off, dst=dst):
                        return s.activation(dst, ps[:, soff + off: soff + 1024],
                                            AF.Exp, scale=SCALE)
                    ec = S.emit("scalar", ef, waits=[("pe", pc)], inc="act")
                    prev_exp[sidx % 2] = ec
                    if diag_mask:
                        return S.emit("vector", lambda v, dst=dst:
                            v.tensor_tensor(dst[:, 0:128], dst[:, 0:128],
                                            tri[:], ALU.mult),
                            waits=[("act", ec), ("gp", setup_gp)], inc="dve")
                    return ec

                def rs_pv(kt, off, pT, ws):
                    nonlocal n_rspv
                    first = (n_rspv == 0)
                    pcs = pieces_of(off)
                    w = list(ws)
                    if first:
                        w.append(("dve", o_free))

                    def mm2(t, kt=kt, pT=pT, pcs=pcs, first=first, vb=vb):
                        last = None
                        for (o, wd) in pcs:
                            t.matmul(ps[0:1, 3072 + o: 3072 + o + wd],
                                     ones_cb[:], pT[:, o:o + wd],
                                     start=first, stop=True)
                            last = t.matmul(ps[:, 2048 + o: 2048 + o + wd],
                                            vb[:, kt * 128:(kt + 1) * 128],
                                            pT[:, o:o + wd], start=first,
                                            stop=True)
                        return last
                    n_rspv += 1
                    return S.emit("tensor", mm2, waits=w, inc="pe")

                tiles = []
                for j in range(8):
                    kt = nfull + j
                    tiles.append((kt, 128 * j,
                                  diag[:, j * 1024:(j + 1) * 1024], True))
                for kt in range(nfull):
                    tiles.append((kt, 0, rot[kt % 3], False))
                done = {}
                for idx in range(len(tiles) + 1):
                    if idx < len(tiles):
                        kt, off, buf, dm = tiles[idx]
                        done[idx] = (s_exp(kt, off, buf[:, off:1024] if dm
                                           else buf[:, 0:1024], diag_mask=dm),
                                     "dve" if dm else "act")
                    if idx == 1:
                        flush_pending()
                    if idx >= 1 and idx - 1 < len(tiles):
                        kt, off, buf, dm = tiles[idx - 1]
                        cnt, sem = done[idx - 1]
                        rs_pv(kt, off, buf[:, 0:1024], [(sem, cnt)])
                pv_done = S.now("pe")
                kv_last[g] = pv_done
                q_free[ui] = pv_done

                c1 = S.emit("vector", lambda v: v.tensor_copy(rs_row[:],
                                                         ps[0:1, 3072:4096]),
                       waits=[("pe", pv_done)], inc="dve")
                S.emit("scalar", lambda s: s.activation(rr_row[:], rs_row[:],
                                                        AF.Ln),
                       waits=[("dve", c1)], inc="act")
                c2 = S.emit("scalar", lambda s: s.activation(
                    rr_row[:], rr_row[:], AF.Exp, scale=-1.0), inc="act")

                def norm_tail(ui=ui, h=h, qc=qc, c2=c2):
                    nonlocal o_free

                    def bcast(t):
                        t.matmul(ps[:, 3072:3584], ones_r32[:],
                                 rr_row[0:1, 0:512], start=True, stop=True)
                        return t.matmul(ps[:, 3584:4096], ones_r32[:],
                                        rr_row[0:1, 512:1024], start=True,
                                        stop=True)
                    bc = S.emit("tensor", bcast, waits=[("act", c2)], inc="pe")
                    cp = S.emit("vector", lambda v: v.tensor_copy(
                        rb_sb, ps[:, 3072:4096]), waits=[("pe", bc)], inc="dve")
                    onr = S.emit("vector", lambda v, ui=ui: v.tensor_tensor(
                        ostg[ui % 2][:], ps[:, 2048:3072], rb_sb, ALU.mult),
                        waits=[("dve", cp), ("sy", ost_free.get(ui - 2, 0))],
                        inc="dve")
                    o_free = onr
                    ost_free[ui] = S.emit("sync", lambda s, h=h, qc=qc, ui=ui:
                        s.dma_start(out=oS[h][:, qc * 1024:(qc + 1) * 1024],
                                    in_=ostg[ui % 2][:]),
                        waits=[("dve", onr)], inc="sy", amt=16)
                pending[0] = norm_tail
            flush_pending()
            o_stores = S.now("sy")
            p3_pe = S.now("pe")
            st["ar_rope_free"] = S.now("dve")

            osl = xb
            wosl = [a_bf(0, 8192), a_bf(16384, 8192)]
            xres = [a_f32(32768, 1024), a_f32(36864, 1024)]
            outst = [a_f32(40960, 1024), a_f32(45056, 1024)]
            outbf = [a_bf(49152, 1024), a_bf(51200, 1024)]
            for f in range(NH):
                S.emit("sync", lambda s, f=f: s.dma_start(
                    out=osl[:, f * T:(f + 1) * T], in_=oS[f]),
                    waits=[("sy", o_stores), ("pe", v_pe_done)],
                    inc="sy", amt=16)
            oslab_ready = S.now("sy")
            wo_mm = {}
            ev_done = {}
            xr_ld = {}
            out_st = {}
            for dg in range(16):
                src3 = wo[:, dg * 256:(dg + 1) * 256].rearrange(
                    "(c p) f -> p c f", p=128)
                wld = S.emit("gpsimd", lambda g, dg=dg, src3=src3: g.dma_start(
                    out=wosl[dg % 2].rearrange("p (c f) -> p c f", f=256),
                    in_=src3),
                    waits=[("pe", wo_mm.get(dg - 2, p3_pe))],
                    inc="gp", amt=16)
                for i in range(2):
                    dout = dg * 2 + i
                    for th in range(2):
                        un = dg * 4 + i * 2 + th
                        pb = (un % 2) * 1024
                        xr_ld[un] = S.emit("sync", lambda s, dout=dout, th=th,
                                           un=un: s.dma_start(
                            out=xres[un % 2],
                            in_=src[dout * 128:(dout + 1) * 128,
                                    th * 1024:(th + 1) * 1024]),
                            waits=[("dve", ev_done.get(un - 2,
                                                         st["ar_rope_free"])),
                                   ("sy", max(src_sy, o_stores))],
                            inc="sy", amt=16)

                        def wmm(t, dg=dg, i=i, th=th, pb=pb):
                            last = None
                            for f in range(NH):
                                for tc in range(2):
                                    last = t.matmul(
                                        ps[:, pb + tc * 512:
                                           pb + (tc + 1) * 512],
                                        wosl[dg % 2][:, f * 256 + i * 128:
                                                     f * 256 + i * 128 + 128],
                                        osl[:, f * T + th * 1024 + tc * 512:
                                            f * T + th * 1024 + (tc + 1) * 512],
                                        start=(f == 0), stop=(f == NH - 1))
                            return last
                        mc = S.emit("tensor", wmm,
                                    waits=[("gp", wld), ("sy", oslab_ready),
                                           ("dve", ev_done.get(un - 2, 0))],
                                    inc="pe")
                        if i == 1 and th == 1:
                            wo_mm[dg] = mc
                        S.emit("vector", lambda v, un=un, pb=pb:
                            v.tensor_add(outst[un % 2], ps[:, pb:pb + 1024],
                                         xres[un % 2]),
                            waits=[("pe", mc), ("sy", xr_ld[un]),
                                   ("sy", out_st.get(un - 2, 0))], inc="dve")
                        ev_done[un] = S.emit("vector", lambda v, un=un:
                            v.tensor_copy(outbf[un % 2], outst[un % 2]),
                            inc="dve") if dst_bf is not None else                             S.now("dve")
                        def stf(s, dout=dout, th=th, un=un):
                            sy = sems["sy"]
                            s.dma_start(
                                out=dst[dout * 128:(dout + 1) * 128,
                                        th * 1024:(th + 1) * 1024],
                                in_=outst[un % 2]).then_inc(sy, 16)
                            if dst_bf is not None:
                                s.dma_start(
                                    out=dst_bf[dout * 128:(dout + 1) * 128,
                                               th * 1024:(th + 1) * 1024],
                                    in_=outbf[un % 2]).then_inc(sy, 16)
                        out_st[un] = S.emit("sync", stf,
                            waits=[("dve", ev_done[un])], inc="sy",
                            amt=32 if dst_bf is not None else 16,
                            self_inc=True)
            st["src_ready_sy"] = S.now("sy")
            st["ar_free_stats"] = S.now("dve")
            st["xb_free"] = S.now("pe")

        def emit_ffn(src, dst, src_bf, dst_bf):
            src_sy = st.get("src_ready_sy", 0)
            stt = emit_stats(src, False, src_sy, src_bf)
            R128 = stt["R128"]
            g_sb = a_f32(36864, 2048)[0:8, :]
            g_tok = a_f32(45056, 128)
            m1 = a_f32(45568, 16)
            nm1 = a_f32(45632, 16)
            m2 = a_f32(45696, 16)
            gm = a_f32(45760, 128)
            sel = a_f32(46272, 128)
            eg = a_f32(46784, 128)
            seg = a_f32(47296, 128)
            den = a_f32(47808, 16)
            rden = a_f32(47872, 16)
            wtok = a_f32(47936, 128)
            wfT = a_f32(48448, 2048)[0:8, :]
            W128 = a_f32(57344, 2048)
            low_w = a_bf(65536, 2048)
            gw_sb = a_bf(69632, 256)
            A_sb = a_bf(36864, 4096)
            B_sb = a_bf(20480, 4096)
            arf = st.get("ar_ffn_free", 0)
            gl = S.emit("gpsimd", lambda g: g.dma_start(
                out=gw_sb.rearrange("p (c f) -> p c f", f=8),
                in_=gwT.rearrange("(c p) f -> p c f", p=128)),
                waits=[("dve", arf)], inc="gp", amt=16)
            bl = S.emit("gpsimd", lambda g: g.dma_start(
                out=B_sb.rearrange("p (c f) -> p c f", f=128),
                in_=bT.rearrange("p (c f) -> p c f", f=128)),
                waits=[("dve", stt["R128_evac"])], inc="gp", amt=16)

            def gmm(t):
                last = None
                for d in range(NCD):
                    for tc in range(4):
                        last = t.matmul(
                            ps[0:8, tc * 512:(tc + 1) * 512],
                            gw_sb[:, d * 8:(d + 1) * 8],
                            xb[:, d * T + tc * 512: d * T + (tc + 1) * 512],
                            start=(d == 0), stop=(d == NCD - 1))
                return last
            gp_mm = S.emit("tensor", gmm, waits=[("gp", gl),
                                                 ("dve", stt["xc_done"])],
                           inc="pe")
            gev = S.emit("vector", lambda v: v.tensor_tensor(
                g_sb, ps[0:8, 0:2048], R128[0:8, :], ALU.mult),
                waits=[("pe", gp_mm), ("dve", stt["R128_evac"])], inc="dve")

            def gtr(t):
                last = None
                for i in range(16):
                    last = t.transpose(ps[:, 2048 + i * 8: 2048 + (i + 1) * 8],
                                       g_sb[:, i * 128:(i + 1) * 128],
                                       ident[0:8, 0:8])
                return last
            gt = S.emit("tensor", gtr, waits=[("dve", gev), ("sy", setup_sy)],
                        inc="pe")
            gtc = S.emit("vector", lambda v: v.tensor_copy(g_tok,
                                                           ps[:, 2048:2176]),
                         waits=[("pe", gt)], inc="dve")
            al = S.emit("gpsimd", lambda g: g.dma_start(
                out=A_sb.rearrange("p (c f) -> p c f", f=128),
                in_=aT.rearrange("(c p) f -> p c f", p=128)),
                waits=[("pe", gt)], inc="gp", amt=16)

            def emit_gate_tail():
                g3 = g_tok.rearrange("p (i e) -> p i e", e=8)

                def b3(x):
                    return x.rearrange("p (i e) -> p i e", e=8)
                S.emit("vector", lambda v: v.tensor_reduce(
                    m1, g3, mybir.AxisListType.X, ALU.max),
                    waits=[("dve", gtc)], inc="dve")
                m1b = m1.unsqueeze(2).broadcast_to([128, 16, 8])
                S.emit("vector", lambda v: v.tensor_tensor(
                    b3(sel), g3, m1b, ALU.is_ge), inc="dve")
                S.emit("vector", lambda v: v.scalar_tensor_tensor(
                    b3(gm), b3(sel), -1e30, g3, ALU.mult, ALU.add), inc="dve")
                S.emit("vector", lambda v: v.tensor_reduce(
                    m2, b3(gm), mybir.AxisListType.X, ALU.max), inc="dve")
                m2b = m2.unsqueeze(2).broadcast_to([128, 16, 8])
                S.emit("vector", lambda v: v.tensor_tensor(
                    b3(sel), g3, m2b, ALU.is_ge), inc="dve")
                m1b2 = m1.unsqueeze(2).broadcast_to([128, 16, 8])
                gs = S.emit("vector", lambda v: v.tensor_tensor(
                    b3(gm), g3, m1b2, ALU.subtract), inc="dve")
                ee = S.emit("scalar", lambda s: s.activation(eg, gm, AF.Exp),
                            waits=[("dve", gs)], inc="act")
                S.emit("vector", lambda v: v.tensor_tensor(seg, sel, eg,
                                                           ALU.mult),
                       waits=[("act", ee)], inc="dve")
                dn = S.emit("vector", lambda v: v.tensor_reduce(
                    den, b3(seg), mybir.AxisListType.X, ALU.add), inc="dve")
                S.emit("scalar", lambda s: s.activation(nm1, den, AF.Ln),
                       waits=[("dve", dn)], inc="act")
                rd = S.emit("scalar", lambda s: s.activation(
                    rden, nm1, AF.Exp, scale=-1.0), inc="act")
                rdb = rden.unsqueeze(2).broadcast_to([128, 16, 8])
                wt = S.emit("vector", lambda v: v.tensor_tensor(
                    b3(wtok), b3(seg), rdb, ALU.mult), waits=[("act", rd)],
                    inc="dve")

                def wtr(t):
                    last = None
                    for i in range(16):
                        last = t.transpose(
                            ps[0:8, 2048 + i * 128: 2048 + (i + 1) * 128],
                            wtok[:, i * 8:(i + 1) * 8], ident[:])
                    return last
                wtp = S.emit("tensor", wtr, waits=[("dve", wt)], inc="pe")
                wfc = S.emit("vector", lambda v: v.tensor_copy(
                    wfT, ps[0:8, 2048:4096]), waits=[("pe", wtp)], inc="dve")

                def ebc(t):
                    last = None
                    for tc in range(4):
                        last = t.matmul(
                            ps[:, 2048 + tc * 512: 2048 + (tc + 1) * 512],
                            e8[:], wfT[0:8, tc * 512:(tc + 1) * 512],
                            start=True, stop=True)
                    return last
                eb = S.emit("tensor", ebc,
                            waits=[("dve", wfc), ("sy", setup_sy)], inc="pe")
                w128c = S.emit("vector", lambda v: v.tensor_copy(
                    W128, ps[:, 2048:4096]), waits=[("pe", eb)], inc="dve")

                def lmm(t):
                    last = None
                    for d in range(NCD):
                        for tc in range(4):
                            last = t.matmul(
                                ps[:, 2048 + tc * 512: 2048 + (tc + 1) * 512],
                                A_sb[:, d * 128:(d + 1) * 128],
                                xb[:, d * T + tc * 512: d * T + (tc + 1) * 512],
                                start=(d == 0), stop=(d == NCD - 1))
                    return last
                lo = S.emit("tensor", lmm, waits=[("gp", al),
                                                  ("dve", stt["xc_done"]),
                                                  ("dve", w128c)], inc="pe")
                st["_lwv"] = S.emit("vector", lambda v: v.tensor_tensor(
                    low_w[:], ps[:, 2048:4096], W128, ALU.mult),
                    waits=[("pe", lo), ("dve", w128c)], inc="dve")
                st["_eb"] = eb

            bsl = [a_bf(0, 4096), a_bf(49152, 4096)]
            outbf = [a_bf(57344, 1024), a_bf(59392, 1024)]
            xres = [a_f32(28672, 1024), a_f32(32768, 1024)]
            t1b = a_f32(36864, 1024)
            outst = [a_f32(40960, 1024), a_f32(45056, 1024)]
            ba_mm = {}
            ev_done = {}
            xr_ld = {}
            out_st = {}
            bsl_ld = {}
            base_mm = {}

            def emit_bsl_load(dc, extra):
                src3 = bwT[:, dc * 128:(dc + 1) * 128].rearrange(
                    "(c p) f -> p c f", p=128)
                bsl_ld[dc] = S.emit("gpsimd", lambda g, dc=dc, src3=src3:
                    g.dma_start(out=bsl[dc % 2].rearrange("p (c f) -> p c f",
                                                          f=128), in_=src3),
                    waits=extra, inc="gp", amt=16)

            def emit_base_mm(dc, th, extra):
                un = dc * 2 + th
                pb = (un % 2) * 1024

                def bmm(t, dc=dc, th=th, pb=pb):
                    last = None
                    for d in range(NCD):
                        for tc in range(2):
                            last = t.matmul(
                                ps[:, pb + tc * 512: pb + (tc + 1) * 512],
                                bsl[dc % 2][:, d * 128:(d + 1) * 128],
                                xb[:, d * T + th * 1024 + tc * 512:
                                   d * T + th * 1024 + (tc + 1) * 512],
                                start=(d == 0), stop=False)
                    return last
                base_mm[un] = S.emit("tensor", bmm,
                                     waits=[("gp", bsl_ld[dc]),
                                            ("dve", stt["xc_done"])] + extra,
                                     inc="pe")

            def emit_lrb_evac(dc, th):
                un = dc * 2 + th
                pb = (un % 2) * 1024
                xr_ld[un] = S.emit("sync", lambda s, dc=dc, th=th, un=un:
                    s.dma_start(out=xres[un % 2],
                                in_=src[dc * 128:(dc + 1) * 128,
                                        th * 1024:(th + 1) * 1024]),
                    waits=[("dve", ev_done.get(un - 2, lwv)),
                           ("sy", src_sy)], inc="sy", amt=16)

                def lrb(t, dc=dc, th=th, pb=pb):
                    last = None
                    for tc in range(2):
                        last = t.matmul(
                            ps[:, pb + tc * 512: pb + (tc + 1) * 512],
                            B_sb[:, dc * 128:(dc + 1) * 128],
                            low_w[:, th * 1024 + tc * 512:
                                  th * 1024 + (tc + 1) * 512],
                            start=False, stop=True)
                    return last
                mc = S.emit("tensor", lrb,
                            waits=[("gp", bl), ("dve", lwv)], inc="pe")
                if th == 1:
                    ba_mm[dc] = mc
                return mc, un

            emit_bsl_load(0, [("pe", stt["stats_pe"])])
            emit_base_mm(0, 0, [("dve", gev)])
            emit_base_mm(0, 1, [])
            emit_gate_tail()
            lwv = st.pop("_lwv")
            eb_cnt = st.pop("_eb")
            for dc in range(NCD):
                if dc == 1:
                    emit_bsl_load(1, [("pe", eb_cnt)])
                elif dc >= 2:
                    emit_bsl_load(dc, [("pe", ba_mm[dc - 2])])
                for th in range(2):
                    un = dc * 2 + th
                    if dc >= 1:
                        emit_base_mm(dc, th,
                                     [("dve", ev_done.get(un - 2, 0))])
                    mc, un = emit_lrb_evac(dc, th)
                    pb = (un % 2) * 1024
                    S.emit("vector", lambda v, pb=pb, th=th: v.tensor_tensor(
                        t1b, ps[:, pb:pb + 1024],
                        R128[:, th * 1024:(th + 1) * 1024], ALU.mult),
                        waits=[("pe", mc), ("sy", xr_ld[un]),
                               ("sy", out_st.get(un - 2, 0))], inc="dve")
                    S.emit("vector", lambda v, un=un, dc=dc:
                        v.scalar_tensor_tensor(outst[un % 2], t1b,
                                               bbcols[:, dc:dc + 1],
                                               xres[un % 2], ALU.add, ALU.add),
                        waits=[("dve", st["bb_done"])], inc="dve")
                    ev_done[un] = S.emit("vector", lambda v, un=un:
                        v.tensor_copy(outbf[un % 2], outst[un % 2]),
                        inc="dve") if dst_bf is not None else S.now("dve")
                    def stf(s, dc=dc, th=th, un=un):
                        sy = sems["sy"]
                        s.dma_start(out=dst[dc * 128:(dc + 1) * 128,
                                            th * 1024:(th + 1) * 1024],
                                    in_=outst[un % 2]).then_inc(sy, 16)
                        if dst_bf is not None:
                            s.dma_start(out=dst_bf[dc * 128:(dc + 1) * 128,
                                                   th * 1024:(th + 1) * 1024],
                                        in_=outbf[un % 2]).then_inc(sy, 16)
                    out_st[un] = S.emit("sync", stf,
                        waits=[("dve", ev_done[un])], inc="sy",
                        amt=32 if dst_bf is not None else 16, self_inc=True)
            st["src_ready_sy"] = S.now("sy")
            st["ar_free_stats"] = S.now("dve")
            st["ar_ffn_free"] = S.now("dve")
            st["ar_rope_free"] = S.now("dve")
            st["xb_free"] = S.now("pe")

        streams = [xT, sA, sB, sA]
        dsts = [sA, sB, sA, outT]
        bf_srcs = [None, xbA, xbB, xbA]
        bf_dsts = [xbA, xbB, xbA, None]
        if n_sub < 4:
            dsts[n_sub - 1] = outT
            bf_dsts[n_sub - 1] = None
        specs = [
            ("attn", dict(qk_parts=[(wqkv, 0, 16, qS, 0), (wqkv, 4096, 4, kS, 0)],
                          v_part=(wqkv, 5120), wo=wo_a)),
            ("ffn", None),
            ("attn", dict(qk_parts=[(wq_b, 0, 16, qS, 0), (wk_b, 0, 4, kS, 0)],
                          v_part=(wv_b, 0), wo=wo_b)),
            ("ffn", None),
        ]
        for li in range(n_sub):
            kind, cfg = specs[li]
            if kind == "attn":
                emit_attn(streams[li], dsts[li], cfg["qk_parts"],
                          cfg["v_part"], cfg["wo"], bf_srcs[li], bf_dsts[li])
            else:
                emit_ffn(streams[li], dsts[li], bf_srcs[li], bf_dsts[li])

        S.emit("sync", lambda s: s.nop(), waits=[("sy", S.now("sy"))])
        S.emit("gpsimd", lambda g: g.nop(), waits=[("gp", S.now("gp"))])

        block = es.enter_context(nc.Block())

        def replay(name, eng):
            for fn, waits, inc, amt, self_inc in S.ops[name]:
                for (s, v) in waits:
                    eng.wait_ge(sems[s], v)
                inst = fn(eng)
                if self_inc:
                    continue
                if inc is not None:
                    if inst is None:
                        inst = eng.nop()
                    inst.then_inc(sems[inc], amt)

        block.gpsimd(lambda g: replay("gpsimd", g))
        block.tensor(lambda t: replay("tensor", t))
        block.scalar(lambda s: replay("scalar", s))
        block.vector(lambda v: replay("vector", v))
        block.sync(lambda sy: replay("sync", sy))
    return nc


def _deinterleave_pair(w, nheads):
    cols = []
    for c in range(nheads):
        pair, parity = c // 2, c % 2
        for half in (0, 1):
            head = pair * 2 + half
            cols.append(head * 128 + parity + 2 * np.arange(64))
    return np.ascontiguousarray(w[:, np.concatenate(cols)])


def _host_inputs(w_qkv, w_o_a, wq_b, wk_b, wv_b, wo_b, gate_w, base_w, base_b,
                 lora_A, lora_B):
    f32 = lambda a: np.ascontiguousarray(np.asarray(a, dtype=np.float32))
    wqkv = f32(w_qkv).copy()
    wqkv[:, 0:4096] = _deinterleave_pair(wqkv[:, 0:4096], 32)
    wqkv[:, 4096:5120] = _deinterleave_pair(wqkv[:, 4096:5120], 8)
    out = {
        "wqkv": wqkv,
        "wo_a": f32(w_o_a),
        "wq_b": _deinterleave_pair(f32(wq_b), 32),
        "wk_b": _deinterleave_pair(f32(wk_b), 8),
        "wv_b": f32(wv_b),
        "wo_b": f32(wo_b),
        "bwT": np.ascontiguousarray(f32(base_w).T),
        "bb1": f32(base_b).reshape(1, D),
        "gwT": np.ascontiguousarray(f32(gate_w).T),
        "aT": np.ascontiguousarray(f32(lora_A).reshape(128, D).T),
        "bT": np.ascontiguousarray(f32(lora_B).transpose(0, 2, 1).reshape(128, D)),
    }
    inv = 1.0 / (10000.0 ** (np.arange(0, 128, 2, dtype=np.float64) / 128.0))
    ang = np.arange(T, dtype=np.float64)[:, None] * inv[None, :]
    c = np.cos(ang).T.astype(np.float32)
    s = np.sin(ang).T.astype(np.float32)
    out["cdup"] = np.ascontiguousarray(np.vstack([c, c]))
    out["sdup"] = np.ascontiguousarray(np.vstack([s, s]))
    kk, qq = np.meshgrid(np.arange(128), np.arange(128), indexing="ij")
    out["tri"] = (kk <= qq).astype(np.float32)
    out["maskb"] = np.where(kk <= qq, 0.0, -1e30).astype(np.float32)
    out["ident"] = np.eye(128, dtype=np.float32)
    e8 = np.zeros((8, 128), dtype=np.float32)
    for e in range(8):
        e8[e, 16 * e:16 * (e + 1)] = 2.0
    out["e8"] = e8
    return out


def kernel(L0_attn_a_x, L0_attn_a_w_qkv, L0_attn_a_w_o, L0_attn_b_wq,
           L0_attn_b_wk, L0_attn_b_wv, L0_attn_b_wo, L0_ffn_gate_w,
           L0_ffn_base_w, L0_ffn_base_b, L0_ffn_lora_A, L0_ffn_lora_B):
    shared = _host_inputs(L0_attn_a_w_qkv, L0_attn_a_w_o, L0_attn_b_wq,
                          L0_attn_b_wk, L0_attn_b_wv, L0_attn_b_wo,
                          L0_ffn_gate_w, L0_ffn_base_w, L0_ffn_base_b,
                          L0_ffn_lora_A, L0_ffn_lora_B)
    x = np.asarray(L0_attn_a_x, dtype=np.float32)
    in_maps = []
    for b in range(8):
        m = dict(shared)
        m["xT"] = np.ascontiguousarray(x[b].T)
        in_maps.append(m)
    nc = build_program(N_SUB)
    res = run_bass_kernel_spmd(nc, in_maps, core_ids=list(range(8)))
    out = np.stack([np.ascontiguousarray(np.asarray(r["outT"]).T)
                    for r in res.results])
    return out.astype(np.float32)


# version 11
# speedup vs baseline: 1.1341x; 1.1341x over previous
import numpy as np
import ml_dtypes
import contextlib
import concourse.bass as bass
import concourse.mybir as mybir
from concourse.bass_utils import run_bass_kernel_spmd

F32 = mybir.dt.float32
BF16 = mybir.dt.bfloat16
AF = mybir.ActivationFunctionType
ALU = mybir.AluOpType

D = 4096
T = 2048
NCD = D // 128
NH = 32
NKV = 8
EPS = 1e-5
SCALE = 1.0 / float(np.sqrt(128.0))
N_SUB = 4


class Sched:
    ENGINES = ("sync", "tensor", "vector", "scalar", "gpsimd")

    def __init__(self):
        self.ops = {e: [] for e in self.ENGINES}
        self.cnt = {}

    def emit(self, eng, fn, waits=(), inc=None, amt=1, self_inc=False):
        ws = tuple((s, v) for (s, v) in waits if v > 0)
        if inc is not None:
            self.cnt[inc] = self.cnt.get(inc, 0) + amt
        self.ops[eng].append((fn, ws, inc, amt, self_inc))
        return self.cnt.get(inc)

    def now(self, s):
        return self.cnt.get(s, 0)


def pieces_of(off, width=1024):
    out = []
    while off < width:
        end = min(width, (off // 512 + 1) * 512)
        out.append((off, end - off))
        off = end
    return out


def build_program(n_sub=N_SUB):
    nc = bass.Bass("TRN2", target_bir_lowering=False)
    dr = lambda name, shape, dt, kind: nc.dram_tensor(name, shape, dt, kind=kind).ap()
    xT = dr("xT", [D, T], F32, "ExternalInput")
    wqkv = dr("wqkv", [D, 6144], F32, "ExternalInput")
    wo_a = dr("wo_a", [D, D], F32, "ExternalInput")
    wq_b = dr("wq_b", [D, D], F32, "ExternalInput")
    wk_b = dr("wk_b", [D, 1024], F32, "ExternalInput")
    wv_b = dr("wv_b", [D, 1024], F32, "ExternalInput")
    wo_b = dr("wo_b", [D, D], F32, "ExternalInput")
    bwT = dr("bwT", [D, D], F32, "ExternalInput")
    bb1 = dr("bb1", [1, D], F32, "ExternalInput")
    gwT = dr("gwT", [D, 8], F32, "ExternalInput")
    aT = dr("aT", [D, 128], F32, "ExternalInput")
    bT = dr("bT", [128, D], F32, "ExternalInput")
    cdupd = dr("cdup", [128, T], F32, "ExternalInput")
    sdupd = dr("sdup", [128, T], F32, "ExternalInput")
    trid = dr("tri", [128, 128], F32, "ExternalInput")
    maskbd = dr("maskb", [128, 128], F32, "ExternalInput")
    identd = dr("ident", [128, 128], F32, "ExternalInput")
    e8d = dr("e8", [8, 128], F32, "ExternalInput")
    outT = dr("outT", [D, T], F32, "ExternalOutput")
    sA = dr("sA", [D, T], F32, "Internal")
    sB = dr("sB", [D, T], F32, "Internal")
    xbA = dr("xbA", [D, T], BF16, "Internal")
    xbB = dr("xbB", [D, T], BF16, "Internal")
    qS = dr("qS", [NH, 128, T], BF16, "Internal")
    kS = dr("kS", [NKV, 128, T], BF16, "Internal")
    vS = dr("vS", [NKV, T, 128], BF16, "Internal")
    oS = dr("oS", [NH, 128, T], BF16, "Internal")

    S = Sched()
    es = contextlib.ExitStack()
    with es:
        sb = lambda name, shape, dt: es.enter_context(nc.sbuf_tensor(name, shape, dt))
        xb = sb("xb", [128, NCD * T], BF16)
        arena = sb("arena", [128, 36864], BF16)
        tri = sb("tri_s", [128, 128], BF16)
        maskb = sb("maskb_s", [128, 128], F32)
        ident = sb("ident_s", [128, 128], F32)
        ones_cb = sb("ones_cb", [128, 1], BF16)
        ones_r32 = sb("ones_r32", [1, 128], F32)
        ones_rb = sb("ones_rb", [1, 128], BF16)
        e8 = sb("e8_s", [8, 128], F32)
        bbcols = sb("bbcols", [128, 32], F32)
        r_col = sb("r_col", [128, 16], F32)
        ps = es.enter_context(nc.psum_tensor("ps", [128, 4096], F32))
        sems = {n: es.enter_context(nc.semaphore(n))
                for n in ("gp", "sy", "pe", "act", "dve", "pl")}

        ar_f = arena.ap().bitcast(F32)

        def a_bf(off, cols):
            assert off % 2 == 0 and off // 2 + cols <= 36864, (off, cols)
            return arena[:, off // 2: off // 2 + cols]

        def a_f32(off, cols):
            assert off % 4 == 0 and off // 4 + cols <= 18432, (off, cols)
            return ar_f[:, off // 4: off // 4 + cols]

        sx_row = a_f32(53248, 2048)[0:1, :]
        msq_row = sx_row
        mbf_row = a_bf(53248, 2048)[0:1, :]
        sxx_row = a_f32(61440, 2048)[0:1, :]
        rstd_row = sxx_row
        m_row = a_f32(20480, 2048)[0:1, :]
        var_row = a_f32(28672, 2048)[0:1, :]
        rs_row = a_f32(53248, 1024)[0:1, :]
        rr_row = a_f32(57344, 1024)[0:1, :]
        rr_bf = a_bf(51200, 1024)[0:1, :]
        bbrow = a_f32(53248, 4096)[0:1, :]

        S.emit("gpsimd", lambda g: g.dma_start(out=tri[:], in_=trid), inc="gp", amt=16)
        S.emit("gpsimd", lambda g: g.dma_start(out=maskb[:], in_=maskbd), inc="gp", amt=16)
        S.emit("sync", lambda s: s.dma_start(out=ident[:], in_=identd), inc="sy", amt=16)
        S.emit("sync", lambda s: s.dma_start(out=e8[:], in_=e8d), inc="sy", amt=16)
        S.emit("gpsimd", lambda g: g.memset(ones_cb[:], 1.0), inc="gp", amt=1)
        S.emit("gpsimd", lambda g: g.memset(ones_r32[:], 1.0), inc="gp", amt=1)
        S.emit("gpsimd", lambda g: g.memset(ones_rb[:], 1.0), inc="gp", amt=1)
        S.emit("sync", lambda s: s.dma_start(out=bbrow, in_=bb1),
               inc="sy", amt=16)
        setup_sy = S.now("sy")
        setup_gp = S.now("gp")

        def bb_tr(t):
            last = None
            for c in range(32):
                last = t.transpose(ps[:, 3584 + c: 3585 + c],
                                   bbrow[:, c * 128:(c + 1) * 128],
                                   ident[0:1, 0:1])
            return last
        bbp = S.emit("tensor", bb_tr, waits=[("sy", setup_sy), ("gp", setup_gp)],
                     inc="pe")
        bbd = S.emit("vector", lambda v: v.tensor_copy(bbcols[:], ps[:, 3584:3616]),
                     waits=[("pe", bbp)], inc="dve")
        st = {"bb_done": bbd}

        def emit_stats(src, want_rcol, src_sy, src_bf=None):
            xsq = [a_bf(0, 2048), a_bf(4096, 2048)]
            M128 = a_bf(8192, 2048)
            R128 = a_f32(12288, 2048)
            xb_free = st.get("xb_free", 0)
            ar_free = st.get("ar_free_stats", 0)
            ld = {}
            if src_bf is None:
                for c in range(NCD):
                    ld[c] = S.emit("gpsimd", lambda g, c=c: g.dma_start(
                        out=xb[:, c * T:(c + 1) * T],
                        in_=src[c * 128:(c + 1) * 128, :]),
                        waits=[("pe", xb_free), ("sy", src_sy)],
                        inc="gp", amt=16)
            else:
                for grp in range(NCD // 4):
                    cc = S.emit("sync", lambda s, grp=grp: s.dma_start(
                        out=xb[:, grp * 4 * T:(grp + 1) * 4 * T].rearrange(
                            "p (c t) -> p c t", t=T),
                        in_=src_bf[grp * 512:(grp + 1) * 512, :].rearrange(
                            "(c p) t -> p c t", p=128)),
                        waits=[("pe", xb_free), ("sy", src_sy)],
                        inc="sy", amt=16)
                    for c in range(grp * 4, grp * 4 + 4):
                        ld[c] = ("sy", cc)
            def _ldw(c):
                v = ld[c]
                return v if isinstance(v, tuple) else ("gp", v)
            sq = {}
            mmc = {}
            for c in range(NCD):
                sqw = [_ldw(c), ("dve", ar_free)]
                if c >= 2:
                    sqw.append(("pe", mmc[c - 2]))
                sq[c] = S.emit("scalar", lambda s, c=c: s.activation(
                    xsq[c % 2][:], xb[:, c * T:(c + 1) * T], AF.Square),
                    waits=sqw, inc="act")

                def statmm(t, c=c):
                    last = None
                    for tc in range(4):
                        t.matmul(ps[0:1, tc * 512:(tc + 1) * 512], ones_cb[:],
                                 xb[:, c * T + tc * 512: c * T + (tc + 1) * 512],
                                 start=(c == 0), stop=(c == NCD - 1))
                        last = t.matmul(
                            ps[0:1, 2048 + tc * 512: 2048 + (tc + 1) * 512],
                            ones_cb[:], xsq[c % 2][:, tc * 512:(tc + 1) * 512],
                            start=(c == 0), stop=(c == NCD - 1))
                    return last
                mmc[c] = S.emit("tensor", statmm,
                                waits=[_ldw(c), ("gp", setup_gp),
                                       ("act", sq[c])], inc="pe")
            stats_done = mmc[NCD - 1]
            S.emit("vector", lambda v: v.tensor_copy(sx_row[:], ps[0:1, 0:2048]),
                   waits=[("pe", stats_done)], inc="dve")
            S.emit("vector", lambda v: v.tensor_copy(sxx_row[:], ps[0:1, 2048:4096]),
                   inc="dve")
            S.emit("vector", lambda v: v.tensor_scalar_mul(m_row[:], sx_row[:],
                                                           1.0 / D), inc="dve")
            S.emit("vector", lambda v: v.tensor_tensor(msq_row[:], m_row[:],
                                                       m_row[:], ALU.mult),
                   inc="dve")
            S.emit("vector", lambda v: v.tensor_scalar_mul(var_row[:], sxx_row[:],
                                                           1.0 / D), inc="dve")
            S.emit("vector", lambda v: v.tensor_sub(var_row[:], var_row[:],
                                                    msq_row[:]), inc="dve")
            vv = S.emit("vector", lambda v: v.tensor_scalar_add(
                var_row[:], var_row[:], EPS), inc="dve")
            S.emit("scalar", lambda s: s.activation(rstd_row[:], var_row[:],
                                                    AF.Ln),
                   waits=[("dve", vv)], inc="act")
            l2 = S.emit("scalar", lambda s: s.activation(rstd_row[:], rstd_row[:],
                                                         AF.Exp, scale=-0.5),
                        inc="act")
            mb = S.emit("vector", lambda v: v.tensor_copy(mbf_row[:], m_row[:]),
                        waits=[("act", l2)], inc="dve")

            def bcasts(t):
                last = None
                for tc in range(4):
                    t.matmul(ps[:, tc * 512:(tc + 1) * 512], ones_rb[:],
                             mbf_row[0:1, tc * 512:(tc + 1) * 512],
                             start=True, stop=True)
                    last = t.matmul(ps[:, 2048 + tc * 512: 2048 + (tc + 1) * 512],
                                    ones_r32[:],
                                    rstd_row[0:1, tc * 512:(tc + 1) * 512],
                                    start=True, stop=True)
                return last
            bc = S.emit("tensor", bcasts, waits=[("dve", mb)], inc="pe")
            e1 = S.emit("vector", lambda v: v.tensor_copy(M128[:], ps[:, 0:2048]),
                        waits=[("pe", bc)], inc="dve")
            e2 = S.emit("vector", lambda v: v.tensor_copy(R128[:], ps[:, 2048:4096]),
                        inc="dve")
            rc = e2
            if want_rcol:
                def rctr(t):
                    last = None
                    for i in range(16):
                        last = t.transpose(ps[:, 3584 + i:3585 + i],
                                           rstd_row[0:1, i * 128:(i + 1) * 128],
                                           ident[0:1, 0:1])
                    return last
                rt = S.emit("tensor", rctr, waits=[("dve", e2)], inc="pe")
                rc = S.emit("vector", lambda v: v.tensor_copy(
                    r_col[:], ps[:, 3584:3600]), waits=[("pe", rt)], inc="dve")
            xcd = 0
            for c in range(NCD):
                xcd = S.emit("vector", lambda v, c=c: v.tensor_sub(
                    xb[:, c * T:(c + 1) * T], xb[:, c * T:(c + 1) * T], M128[:]),
                    waits=[("dve", rc), ("pe", stats_done)], inc="dve")
            return dict(xc_done=xcd, R128=R128, rcol_done=rc, R128_evac=e2,
                        stats_pe=stats_done)

        def emit_attn(src, dst, qk_parts, v_part, wo, src_bf, dst_bf):
            src_sy = st.get("src_ready_sy", 0)
            stt = emit_stats(src, True, src_sy, src_bf)
            R128 = stt["R128"]
            cR2 = a_bf(36864, 2048)
            sR2 = a_bf(40960, 2048)
            cd = a_f32(45056, 2048)
            sd = a_f32(53248, 2048)
            S.emit("sync", lambda s: s.dma_start(out=cd, in_=cdupd),
                   waits=[("dve", st.get("ar_rope_free", 0)),
                          ("dve", stt["R128_evac"])], inc="sy", amt=16)
            tl2 = S.emit("sync", lambda s: s.dma_start(out=sd, in_=sdupd),
                         waits=[("dve", stt["R128_evac"])], inc="sy", amt=16)
            tb1 = S.emit("vector", lambda v: v.tensor_tensor(cR2[:], cd, R128[:],
                                                             ALU.mult),
                         waits=[("sy", tl2), ("dve", stt["R128_evac"])],
                         inc="dve")
            tables_done = S.emit("vector", lambda v: v.tensor_tensor(
                sR2[:], sd, R128[:], ALU.mult), inc="dve")

            wsl = [a_bf(0, 8192), a_bf(16384, 8192)]
            u1 = a_bf(61440, 1024)
            u2 = a_bf(63488, 1024)
            stg = [a_bf(65536, 1024), a_bf(67584, 1024),
                   a_bf(69632, 1024), a_bf(71680, 1024)]
            pair_list = []
            for (w_ap, coff, npair, scr, hb) in qk_parts:
                for p in range(npair):
                    pair_list.append((w_ap, coff + p * 256, scr, hb + 2 * p))
            mm_done = {}
            rope_done = {}
            stage_free = {}
            slab_ready = {}
            for ui, (w_ap, coff, scr, h0) in enumerate(pair_list):
                src3 = w_ap[:, coff:coff + 256].rearrange("(c p) f -> p c f", p=128)
                ws = [("dve", tables_done)] if ui < 2 else \
                     [("pe", mm_done[ui - 2])]
                slab_ready[ui] = S.emit("gpsimd", lambda g, ui=ui, src3=src3:
                    g.dma_start(out=wsl[ui % 2].rearrange("p (c f) -> p c f",
                                                          f=256), in_=src3),
                    waits=ws, inc="gp", amt=16)
                for th in range(2):
                    un = ui * 2 + th
                    base = (un % 2) * 2048
                    pE, pO = base, base + 1024

                    def mms(t, ui=ui, th=th, pE=pE, pO=pO):
                        last = None
                        sl = wsl[ui % 2]
                        for d in range(NCD):
                            for (po, ofs) in ((pE, 0), (pO, 128)):
                                for tc in range(2):
                                    last = t.matmul(
                                        ps[:, po + tc * 512: po + (tc + 1) * 512],
                                        sl[:, d * 256 + ofs: d * 256 + ofs + 128],
                                        xb[:, d * T + th * 1024 + tc * 512:
                                           d * T + th * 1024 + (tc + 1) * 512],
                                        start=(d == 0), stop=(d == NCD - 1))
                        return last
                    mmu = S.emit("tensor", mms,
                                 waits=[("gp", slab_ready[ui]),
                                        ("dve", stt["xc_done"]),
                                        ("dve", rope_done.get(un - 2, 0))],
                                 inc="pe")
                    if th == 1:
                        mm_done[ui] = mmu
                    sg_t = stg[(un % 2) * 2]
                    sg_b = stg[(un % 2) * 2 + 1]
                    cs = cR2[:, th * 1024:(th + 1) * 1024]
                    ss = sR2[:, th * 1024:(th + 1) * 1024]
                    S.emit("vector", lambda v, cs=cs, pE=pE: v.tensor_tensor(
                        u1[:], cs, ps[:, pE:pE + 1024], ALU.mult),
                        waits=[("pe", mmu), ("sy", stage_free.get(un - 2, 0))],
                        inc="dve")
                    S.emit("vector", lambda v, ss=ss, pO=pO: v.tensor_tensor(
                        u2[:], ss, ps[:, pO:pO + 1024], ALU.mult), inc="dve")
                    S.emit("vector", lambda v, sg_t=sg_t: v.tensor_sub(
                        sg_t[:], u1[:], u2[:]), inc="dve")
                    S.emit("vector", lambda v, ss=ss, pE=pE: v.tensor_tensor(
                        u1[:], ss, ps[:, pE:pE + 1024], ALU.mult), inc="dve")
                    S.emit("vector", lambda v, cs=cs, pO=pO: v.tensor_tensor(
                        u2[:], cs, ps[:, pO:pO + 1024], ALU.mult), inc="dve")
                    rope_done[un] = S.emit("vector", lambda v, sg_b=sg_b:
                        v.tensor_add(sg_b[:], u1[:], u2[:]), inc="dve")

                    def store(s, scr=scr, h0=h0, th=th, sg_t=sg_t, sg_b=sg_b):
                        tsl = slice(th * 1024, (th + 1) * 1024)
                        sy = sems["sy"]
                        s.dma_start(out=scr[h0, 0:64, tsl],
                                    in_=sg_t[0:64, :]).then_inc(sy, 16)
                        s.dma_start(out=scr[h0 + 1, 0:64, tsl],
                                    in_=sg_t[64:128, :]).then_inc(sy, 16)
                        s.dma_start(out=scr[h0, 64:128, tsl],
                                    in_=sg_b[0:64, :]).then_inc(sy, 16)
                        s.dma_start(out=scr[h0 + 1, 64:128, tsl],
                                    in_=sg_b[64:128, :]).then_inc(sy, 16)
                    stage_free[un] = S.emit("sync", store,
                                            waits=[("dve", rope_done[un])],
                                            inc="sy", amt=64, self_inc=True)
            last_un = len(pair_list) * 2 - 1
            qk_stores = S.now("sy")
            qk_pe = S.now("pe")

            vw_ap, vcoff = v_part
            vsl = a_bf(0, 16384)
            vstg = [a_bf(61440, 512), a_bf(63488, 512)]
            v_mm = {}
            v_evac = {}
            v_st = {}
            for fh in range(2):
                src3 = vw_ap[:, vcoff + fh * 512: vcoff + (fh + 1) * 512] \
                    .rearrange("(c p) f -> p c f", p=128)
                vw0 = [("pe", qk_pe), ("dve", rope_done[last_un])] if fh == 0 \
                    else [("pe", v_mm[(0, 15)])]
                vsl_ld = S.emit("gpsimd", lambda g, src3=src3: g.dma_start(
                    out=vsl.rearrange("p (c f) -> p c f", f=512), in_=src3),
                    waits=vw0, inc="gp", amt=16)
                for tt in range(16):
                    pb = (tt % 2) * 512

                    def vmm(t, tt=tt, pb=pb):
                        last = None
                        for d in range(NCD):
                            last = t.matmul(
                                ps[:, pb:pb + 512],
                                xb[:, d * T + tt * 128: d * T + (tt + 1) * 128],
                                vsl[:, d * 512:(d + 1) * 512],
                                start=(d == 0), stop=(d == NCD - 1))
                        return last
                    vw = [("gp", vsl_ld), ("dve", stt["xc_done"])]
                    if (fh, tt - 2) in v_evac:
                        vw.append(("dve", v_evac[(fh, tt - 2)]))
                    elif fh > 0 and tt < 2:
                        vw.append(("dve", v_evac[(fh - 1, 14 + tt)]))
                    v_mm[(fh, tt)] = S.emit("tensor", vmm, waits=vw, inc="pe")
                    ev = S.emit("vector", lambda v, tt=tt, pb=pb:
                        v.tensor_scalar(vstg[tt % 2][:], ps[:, pb:pb + 512],
                                        r_col[:, tt:tt + 1], None, ALU.mult),
                        waits=[("pe", v_mm[(fh, tt)]), ("dve", stt["rcol_done"]),
                               ("sy", max(qk_stores, v_st.get((fh, tt - 2), 0)))],
                        inc="dve")
                    v_evac[(fh, tt)] = ev
                    v_st[(fh, tt)] = S.emit("sync", lambda s, fh=fh, tt=tt:
                        s.dma_start(
                            out=vS[fh * 4:(fh + 1) * 4,
                                   tt * 128:(tt + 1) * 128, :].rearrange(
                                       "h p n -> p h n"),
                            in_=vstg[tt % 2].rearrange("p (h n) -> p h n",
                                                       n=128)),
                        waits=[("dve", ev)], inc="sy", amt=16)
            v_pe_done = S.now("pe")
            v_stores = S.now("sy")

            p3_dve0 = S.now("dve")
            p3_act0 = S.now("act")
            kv_ld = {}
            kv_last = {}

            def kvload(s, g, kb, vb):
                sy = sems["sy"]
                s.dma_start(out=kb[:], in_=kS[g]).then_inc(sy, 16)
                s.dma_start(out=vb.rearrange("p (k n) -> p k n", n=128),
                            in_=vS[g].rearrange("(k p) n -> p k n",
                                                p=128)).then_inc(sy, 16)

            def stream_gen(si, heads, sb0, diag_off, rot_off, ostg_off,
                           q_off, kv_off, rs_off, rr_off, rrb_off, rb_off):
                S0 = sb0
                S1 = sb0 + 512
                OB = sb0 + 1024
                RB = sb0 + 1536
                kbuf = [a_bf(kv_off, 2048), a_bf(kv_off + 4096, 2048)]
                vbuf = [a_bf(kv_off + 8192, 2048), a_bf(kv_off + 12288, 2048)]
                qbuf = [a_bf(q_off + i * 1024, 512) for i in range(3)]
                diag = [a_bf(diag_off + i * 1024, 512) for i in range(4)]
                rot = [a_bf(rot_off + i * 1024, 512) for i in range(4)]
                ostg = [a_bf(ostg_off + i * 1024, 512) for i in range(2)]
                rs_r = a_f32(rs_off, 512)[0:1, :]
                rr_r = a_f32(rr_off, 512)[0:1, :]
                rr_b = a_bf(rrb_off, 512)[0:1, :]
                rb = a_f32(rb_off, 512)
                prev_exp = {}
                pend = [None]
                o_free = [S.now("dve")]
                ost_free = {}
                q_free = {}
                q_ld = {}
                sidx = [0]
                units = [(h, qc) for h in heads for qc in range(4)]
                for ui, (h, qc) in enumerate(units):
                    g = h // 4
                    if ui == 0:
                        kv_ld[(si, g)] = S.emit("sync",
                            lambda s, g=g: kvload(s, g, kbuf[0], vbuf[0]),
                            waits=[("sy", max(qk_stores, v_stores)),
                                   ("pe", v_pe_done),
                                   ("dve", rope_done[last_un])],
                            inc="sy", amt=32, self_inc=True)
                        if g + 1 <= heads[-1] // 4:
                            kv_ld[(si, g + 1)] = S.emit("sync",
                                lambda s, g=g: kvload(s, g + 1, kbuf[1],
                                                      vbuf[1]),
                                inc="sy", amt=32, self_inc=True)
                    if qc == 0 and h % 4 == 0 and g >= heads[0] // 4 + 1                             and g + 1 <= heads[-1] // 4:
                        kv_ld[(si, g + 1)] = S.emit("sync",
                            lambda s, g=g: kvload(s, g + 1,
                                                  kbuf[(g + 1) % 2],
                                                  vbuf[(g + 1) % 2]),
                            waits=[("pe", kv_last[(si, g - 1)])],
                            inc="sy", amt=32, self_inc=True)
                    q_ld[ui] = S.emit("sync", lambda s, h=h, qc=qc, ui=ui:
                        s.dma_start(out=qbuf[ui % 3][:],
                                    in_=qS[h][:, qc * 512:(qc + 1) * 512]),
                        waits=[("sy", qk_stores),
                               ("pe", max(q_free.get(ui - 3, 0), v_pe_done))],
                        inc="sy", amt=16)
                    kb, vb = kbuf[g % 2], vbuf[g % 2]
                    qb = qbuf[ui % 3]
                    nfull = 4 * qc
                    tiles = []
                    for j in range(4):
                        tiles.append((nfull + j, 128 * j, diag[j], True))
                    for kt in range(nfull):
                        tiles.append((kt, 0, rot[kt % 4], False))
                    done = {}
                    n_t = len(tiles)
                    for idx in range(n_t + 2):
                        if idx < n_t:
                            kt, off, buf, dm = tiles[idx]
                            sx = sidx[0]
                            sidx[0] += 1
                            soff = [S0, S1][sx % 2]
                            wdt = 512 - off

                            def smm(t, kt=kt, off=off, wdt=wdt, soff=soff,
                                    kb=kb, qb=qb):
                                return t.matmul(
                                    ps[:, soff + off: soff + 512],
                                    kb[:, kt * 128:(kt + 1) * 128],
                                    qb[:, off:512], start=True, stop=True)
                            ws = [("act", prev_exp.get(sx % 2, 0)),
                                  ("sy", max(q_ld[ui], kv_ld[(si, g)]))]
                            if sx < 2:
                                ws.append(("dve", p3_dve0))
                            pc = S.emit("tensor", smm, waits=ws, inc="pe")

                            def ef(s, soff=soff, off=off, buf=buf):
                                return s.activation(
                                    buf[:, off:512],
                                    ps[:, soff + off: soff + 512],
                                    AF.Exp, scale=SCALE)
                            ec = S.emit("scalar", ef, waits=[("pe", pc)],
                                        inc="act")
                            prev_exp[sx % 2] = ec
                            if dm:
                                mk = S.emit("gpsimd", lambda gp_, off=off,
                                            buf=buf: gp_.tensor_tensor(
                                    buf[:, off:off + 128],
                                    buf[:, off:off + 128], tri[:], ALU.mult),
                                    waits=[("act", ec), ("gp", setup_gp)],
                                    inc="pl")
                                done[idx] = ("pl", mk)
                            else:
                                done[idx] = ("act", ec)
                        if idx == 1 and pend[0] is not None:
                            fn = pend[0]
                            pend[0] = None
                            fn()
                        if idx >= 2 and idx - 2 < n_t:
                            kt, off, buf, dm = tiles[idx - 2]
                            sem, cnt = done[idx - 2]
                            first = (idx == 2)
                            w = [(sem, cnt)]
                            if first:
                                w.append(("dve", o_free[0]))

                            def mm2(t, kt=kt, off=off, buf=buf, first=first,
                                    vb=vb):
                                t.matmul(ps[0:1, RB + off: RB + 512],
                                         ones_cb[:], buf[:, off:512],
                                         start=first, stop=True)
                                return t.matmul(
                                    ps[:, OB + off: OB + 512],
                                    vb[:, kt * 128:(kt + 1) * 128],
                                    buf[:, off:512], start=first, stop=True)
                            S.emit("tensor", mm2, waits=w, inc="pe")
                        yield
                    pv_done = S.now("pe")
                    kv_last[(si, g)] = pv_done
                    q_free[ui] = pv_done
                    S.emit("scalar", lambda s: s.activation(
                        rr_r[:], ps[0:1, RB:RB + 512], AF.Ln),
                        waits=[("pe", pv_done)], inc="act")
                    c2 = S.emit("scalar", lambda s: s.activation(
                        rr_b[:], rr_r[:], AF.Exp, scale=-1.0), inc="act")

                    def norm_tail(ui=ui, h=h, qc=qc, c2=c2):
                        bc = S.emit("tensor", lambda t: t.matmul(
                            ps[:, RB:RB + 512], ones_rb[:], rr_b[0:1, :],
                            start=True, stop=True),
                            waits=[("act", c2)], inc="pe")
                        cp = S.emit("vector", lambda v: v.tensor_copy(
                            rb[:], ps[:, RB:RB + 512]), waits=[("pe", bc)],
                            inc="dve")
                        onr = S.emit("vector", lambda v, ui=ui:
                            v.tensor_tensor(ostg[ui % 2][:],
                                            ps[:, OB:OB + 512], rb[:],
                                            ALU.mult),
                            waits=[("dve", cp),
                                   ("sy", ost_free.get(ui - 2, 0))],
                            inc="dve")
                        o_free[0] = onr
                        ost_free[ui] = S.emit("sync", lambda s, h=h, qc=qc,
                                              ui=ui: s.dma_start(
                            out=oS[h][:, qc * 512:(qc + 1) * 512],
                            in_=ostg[ui % 2][:]),
                            waits=[("dve", onr)], inc="sy", amt=16)
                    pend[0] = norm_tail
                while True:
                    if pend[0] is not None:
                        fn = pend[0]
                        pend[0] = None
                        fn()
                    yield

            gA = stream_gen(0, list(range(16)), 0, 32768, 36864, 49152,
                            53248, 0, 59392, 61440, 63488, 64512)
            gB = stream_gen(1, list(range(16, 32)), 2048, 40960, 45056,
                            51200, 56320, 16384, 66560, 68608, 70656, 71680)
            na = nb = True
            total = 16 * 4 * (4 + 4 + 4 + 4 + 10 + 2) * 2
            steps = 0
            done_units = 16 * 4 * 2
            need = sum((4 + 4 * qc) + 2 for qc in range(4)) * 16
            for _ in range(need):
                next(gA)
                next(gB)
            for _ in range(4):
                next(gA)
                next(gB)
            o_stores = S.now("sy")
            p3_pe = S.now("pe")
            st["ar_rope_free"] = S.now("dve")

            osl = xb
            wosl = [a_bf(0, 8192), a_bf(16384, 8192)]
            xres = [a_f32(32768, 1024), a_f32(36864, 1024)]
            outst = [a_f32(40960, 1024), a_f32(45056, 1024)]
            outbf = [a_bf(49152, 1024), a_bf(51200, 1024)]
            for f in range(NH):
                S.emit("sync", lambda s, f=f: s.dma_start(
                    out=osl[:, f * T:(f + 1) * T], in_=oS[f]),
                    waits=[("sy", o_stores), ("pe", v_pe_done)],
                    inc="sy", amt=16)
            oslab_ready = S.now("sy")
            wo_mm = {}
            ev_done = {}
            xr_ld = {}
            out_st = {}
            for dg in range(16):
                src3 = wo[:, dg * 256:(dg + 1) * 256].rearrange(
                    "(c p) f -> p c f", p=128)
                wld = S.emit("gpsimd", lambda g, dg=dg, src3=src3: g.dma_start(
                    out=wosl[dg % 2].rearrange("p (c f) -> p c f", f=256),
                    in_=src3),
                    waits=[("pe", wo_mm.get(dg - 2, p3_pe))],
                    inc="gp", amt=16)
                for i in range(2):
                    dout = dg * 2 + i
                    for th in range(2):
                        un = dg * 4 + i * 2 + th
                        pb = (un % 2) * 1024
                        xr_ld[un] = S.emit("sync", lambda s, dout=dout, th=th,
                                           un=un: s.dma_start(
                            out=xres[un % 2],
                            in_=src[dout * 128:(dout + 1) * 128,
                                    th * 1024:(th + 1) * 1024]),
                            waits=[("dve", ev_done.get(un - 2,
                                                         st["ar_rope_free"])),
                                   ("sy", max(src_sy, o_stores))],
                            inc="sy", amt=16)

                        def wmm(t, dg=dg, i=i, th=th, pb=pb):
                            last = None
                            for f in range(NH):
                                for tc in range(2):
                                    last = t.matmul(
                                        ps[:, pb + tc * 512:
                                           pb + (tc + 1) * 512],
                                        wosl[dg % 2][:, f * 256 + i * 128:
                                                     f * 256 + i * 128 + 128],
                                        osl[:, f * T + th * 1024 + tc * 512:
                                            f * T + th * 1024 + (tc + 1) * 512],
                                        start=(f == 0), stop=(f == NH - 1))
                            return last
                        mc = S.emit("tensor", wmm,
                                    waits=[("gp", wld), ("sy", oslab_ready),
                                           ("dve", ev_done.get(un - 2, 0))],
                                    inc="pe")
                        if i == 1 and th == 1:
                            wo_mm[dg] = mc
                        S.emit("vector", lambda v, un=un, pb=pb:
                            v.tensor_add(outst[un % 2], ps[:, pb:pb + 1024],
                                         xres[un % 2]),
                            waits=[("pe", mc), ("sy", xr_ld[un]),
                                   ("sy", out_st.get(un - 2, 0))], inc="dve")
                        ev_done[un] = S.emit("vector", lambda v, un=un:
                            v.tensor_copy(outbf[un % 2], outst[un % 2]),
                            inc="dve") if dst_bf is not None else                             S.now("dve")
                        def stf(s, dout=dout, th=th, un=un):
                            sy = sems["sy"]
                            s.dma_start(
                                out=dst[dout * 128:(dout + 1) * 128,
                                        th * 1024:(th + 1) * 1024],
                                in_=outst[un % 2]).then_inc(sy, 16)
                            if dst_bf is not None:
                                s.dma_start(
                                    out=dst_bf[dout * 128:(dout + 1) * 128,
                                               th * 1024:(th + 1) * 1024],
                                    in_=outbf[un % 2]).then_inc(sy, 16)
                        out_st[un] = S.emit("sync", stf,
                            waits=[("dve", ev_done[un])], inc="sy",
                            amt=32 if dst_bf is not None else 16,
                            self_inc=True)
            st["src_ready_sy"] = S.now("sy")
            st["ar_free_stats"] = S.now("dve")
            st["xb_free"] = S.now("pe")

        def emit_ffn(src, dst, src_bf, dst_bf):
            src_sy = st.get("src_ready_sy", 0)
            stt = emit_stats(src, False, src_sy, src_bf)
            R128 = stt["R128"]
            g_sb = a_f32(36864, 2048)[0:8, :]
            g_tok = a_f32(45056, 128)
            m1 = a_f32(45568, 16)
            nm1 = a_f32(45632, 16)
            m2 = a_f32(45696, 16)
            gm = a_f32(45760, 128)
            sel = a_f32(46272, 128)
            eg = a_f32(46784, 128)
            seg = a_f32(47296, 128)
            den = a_f32(47808, 16)
            rden = a_f32(47872, 16)
            wtok = a_f32(47936, 128)
            wfT = a_f32(48448, 2048)[0:8, :]
            W128 = a_f32(57344, 2048)
            low_w = a_bf(65536, 2048)
            gw_sb = a_bf(69632, 256)
            A_sb = a_bf(36864, 4096)
            B_sb = a_bf(20480, 4096)
            arf = st.get("ar_ffn_free", 0)
            gl = S.emit("gpsimd", lambda g: g.dma_start(
                out=gw_sb.rearrange("p (c f) -> p c f", f=8),
                in_=gwT.rearrange("(c p) f -> p c f", p=128)),
                waits=[("dve", arf)], inc="gp", amt=16)
            bl = S.emit("gpsimd", lambda g: g.dma_start(
                out=B_sb.rearrange("p (c f) -> p c f", f=128),
                in_=bT.rearrange("p (c f) -> p c f", f=128)),
                waits=[("dve", stt["R128_evac"])], inc="gp", amt=16)

            def gmm(t):
                last = None
                for d in range(NCD):
                    for tc in range(4):
                        last = t.matmul(
                            ps[0:8, tc * 512:(tc + 1) * 512],
                            gw_sb[:, d * 8:(d + 1) * 8],
                            xb[:, d * T + tc * 512: d * T + (tc + 1) * 512],
                            start=(d == 0), stop=(d == NCD - 1))
                return last
            gp_mm = S.emit("tensor", gmm, waits=[("gp", gl),
                                                 ("dve", stt["xc_done"])],
                           inc="pe")
            gev = S.emit("vector", lambda v: v.tensor_tensor(
                g_sb, ps[0:8, 0:2048], R128[0:8, :], ALU.mult),
                waits=[("pe", gp_mm), ("dve", stt["R128_evac"])], inc="dve")

            def gtr(t):
                last = None
                for i in range(16):
                    last = t.transpose(ps[:, 2048 + i * 8: 2048 + (i + 1) * 8],
                                       g_sb[:, i * 128:(i + 1) * 128],
                                       ident[0:8, 0:8])
                return last
            gt = S.emit("tensor", gtr, waits=[("dve", gev), ("sy", setup_sy)],
                        inc="pe")
            gtc = S.emit("vector", lambda v: v.tensor_copy(g_tok,
                                                           ps[:, 2048:2176]),
                         waits=[("pe", gt)], inc="dve")
            al = S.emit("gpsimd", lambda g: g.dma_start(
                out=A_sb.rearrange("p (c f) -> p c f", f=128),
                in_=aT.rearrange("(c p) f -> p c f", p=128)),
                waits=[("pe", gt)], inc="gp", amt=16)

            def emit_gate_tail():
                g3 = g_tok.rearrange("p (i e) -> p i e", e=8)

                def b3(x):
                    return x.rearrange("p (i e) -> p i e", e=8)
                S.emit("vector", lambda v: v.tensor_reduce(
                    m1, g3, mybir.AxisListType.X, ALU.max),
                    waits=[("dve", gtc)], inc="dve")
                m1b = m1.unsqueeze(2).broadcast_to([128, 16, 8])
                S.emit("vector", lambda v: v.tensor_tensor(
                    b3(sel), g3, m1b, ALU.is_ge), inc="dve")
                S.emit("vector", lambda v: v.scalar_tensor_tensor(
                    b3(gm), b3(sel), -1e30, g3, ALU.mult, ALU.add), inc="dve")
                S.emit("vector", lambda v: v.tensor_reduce(
                    m2, b3(gm), mybir.AxisListType.X, ALU.max), inc="dve")
                m2b = m2.unsqueeze(2).broadcast_to([128, 16, 8])
                S.emit("vector", lambda v: v.tensor_tensor(
                    b3(sel), g3, m2b, ALU.is_ge), inc="dve")
                m1b2 = m1.unsqueeze(2).broadcast_to([128, 16, 8])
                gs = S.emit("vector", lambda v: v.tensor_tensor(
                    b3(gm), g3, m1b2, ALU.subtract), inc="dve")
                ee = S.emit("scalar", lambda s: s.activation(eg, gm, AF.Exp),
                            waits=[("dve", gs)], inc="act")
                S.emit("vector", lambda v: v.tensor_tensor(seg, sel, eg,
                                                           ALU.mult),
                       waits=[("act", ee)], inc="dve")
                dn = S.emit("vector", lambda v: v.tensor_reduce(
                    den, b3(seg), mybir.AxisListType.X, ALU.add), inc="dve")
                S.emit("scalar", lambda s: s.activation(nm1, den, AF.Ln),
                       waits=[("dve", dn)], inc="act")
                rd = S.emit("scalar", lambda s: s.activation(
                    rden, nm1, AF.Exp, scale=-1.0), inc="act")
                rdb = rden.unsqueeze(2).broadcast_to([128, 16, 8])
                wt = S.emit("vector", lambda v: v.tensor_tensor(
                    b3(wtok), b3(seg), rdb, ALU.mult), waits=[("act", rd)],
                    inc="dve")

                def wtr(t):
                    last = None
                    for i in range(16):
                        last = t.transpose(
                            ps[0:8, 2048 + i * 128: 2048 + (i + 1) * 128],
                            wtok[:, i * 8:(i + 1) * 8], ident[:])
                    return last
                wtp = S.emit("tensor", wtr, waits=[("dve", wt)], inc="pe")
                wfc = S.emit("vector", lambda v: v.tensor_copy(
                    wfT, ps[0:8, 2048:4096]), waits=[("pe", wtp)], inc="dve")

                def ebc(t):
                    last = None
                    for tc in range(4):
                        last = t.matmul(
                            ps[:, 2048 + tc * 512: 2048 + (tc + 1) * 512],
                            e8[:], wfT[0:8, tc * 512:(tc + 1) * 512],
                            start=True, stop=True)
                    return last
                eb = S.emit("tensor", ebc,
                            waits=[("dve", wfc), ("sy", setup_sy)], inc="pe")
                w128c = S.emit("vector", lambda v: v.tensor_copy(
                    W128, ps[:, 2048:4096]), waits=[("pe", eb)], inc="dve")

                def lmm(t):
                    last = None
                    for d in range(NCD):
                        for tc in range(4):
                            last = t.matmul(
                                ps[:, 2048 + tc * 512: 2048 + (tc + 1) * 512],
                                A_sb[:, d * 128:(d + 1) * 128],
                                xb[:, d * T + tc * 512: d * T + (tc + 1) * 512],
                                start=(d == 0), stop=(d == NCD - 1))
                    return last
                lo = S.emit("tensor", lmm, waits=[("gp", al),
                                                  ("dve", stt["xc_done"]),
                                                  ("dve", w128c)], inc="pe")
                st["_lwv"] = S.emit("vector", lambda v: v.tensor_tensor(
                    low_w[:], ps[:, 2048:4096], W128, ALU.mult),
                    waits=[("pe", lo), ("dve", w128c)], inc="dve")
                st["_eb"] = eb

            bsl = [a_bf(0, 4096), a_bf(49152, 4096)]
            outbf = [a_bf(57344, 1024), a_bf(59392, 1024)]
            xres = [a_f32(28672, 1024), a_f32(32768, 1024)]
            t1b = a_f32(36864, 1024)
            outst = [a_f32(40960, 1024), a_f32(45056, 1024)]
            ba_mm = {}
            ev_done = {}
            xr_ld = {}
            out_st = {}
            bsl_ld = {}
            base_mm = {}

            def emit_bsl_load(dc, extra):
                src3 = bwT[:, dc * 128:(dc + 1) * 128].rearrange(
                    "(c p) f -> p c f", p=128)
                bsl_ld[dc] = S.emit("gpsimd", lambda g, dc=dc, src3=src3:
                    g.dma_start(out=bsl[dc % 2].rearrange("p (c f) -> p c f",
                                                          f=128), in_=src3),
                    waits=extra, inc="gp", amt=16)

            def emit_base_mm(dc, th, extra):
                un = dc * 2 + th
                pb = (un % 2) * 1024

                def bmm(t, dc=dc, th=th, pb=pb):
                    last = None
                    for d in range(NCD):
                        for tc in range(2):
                            last = t.matmul(
                                ps[:, pb + tc * 512: pb + (tc + 1) * 512],
                                bsl[dc % 2][:, d * 128:(d + 1) * 128],
                                xb[:, d * T + th * 1024 + tc * 512:
                                   d * T + th * 1024 + (tc + 1) * 512],
                                start=(d == 0), stop=False)
                    return last
                base_mm[un] = S.emit("tensor", bmm,
                                     waits=[("gp", bsl_ld[dc]),
                                            ("dve", stt["xc_done"])] + extra,
                                     inc="pe")

            def emit_lrb_evac(dc, th):
                un = dc * 2 + th
                pb = (un % 2) * 1024
                xr_ld[un] = S.emit("sync", lambda s, dc=dc, th=th, un=un:
                    s.dma_start(out=xres[un % 2],
                                in_=src[dc * 128:(dc + 1) * 128,
                                        th * 1024:(th + 1) * 1024]),
                    waits=[("dve", ev_done.get(un - 2, lwv)),
                           ("sy", src_sy)], inc="sy", amt=16)

                def lrb(t, dc=dc, th=th, pb=pb):
                    last = None
                    for tc in range(2):
                        last = t.matmul(
                            ps[:, pb + tc * 512: pb + (tc + 1) * 512],
                            B_sb[:, dc * 128:(dc + 1) * 128],
                            low_w[:, th * 1024 + tc * 512:
                                  th * 1024 + (tc + 1) * 512],
                            start=False, stop=True)
                    return last
                mc = S.emit("tensor", lrb,
                            waits=[("gp", bl), ("dve", lwv)], inc="pe")
                if th == 1:
                    ba_mm[dc] = mc
                return mc, un

            emit_bsl_load(0, [("pe", stt["stats_pe"])])
            emit_base_mm(0, 0, [("dve", gev)])
            emit_base_mm(0, 1, [])
            emit_gate_tail()
            lwv = st.pop("_lwv")
            eb_cnt = st.pop("_eb")
            for dc in range(NCD):
                if dc == 1:
                    emit_bsl_load(1, [("pe", eb_cnt)])
                elif dc >= 2:
                    emit_bsl_load(dc, [("pe", ba_mm[dc - 2])])
                for th in range(2):
                    un = dc * 2 + th
                    if dc >= 1:
                        emit_base_mm(dc, th,
                                     [("dve", ev_done.get(un - 2, 0))])
                    mc, un = emit_lrb_evac(dc, th)
                    pb = (un % 2) * 1024
                    S.emit("vector", lambda v, pb=pb, th=th: v.tensor_tensor(
                        t1b, ps[:, pb:pb + 1024],
                        R128[:, th * 1024:(th + 1) * 1024], ALU.mult),
                        waits=[("pe", mc), ("sy", xr_ld[un]),
                               ("sy", out_st.get(un - 2, 0))], inc="dve")
                    S.emit("vector", lambda v, un=un, dc=dc:
                        v.scalar_tensor_tensor(outst[un % 2], t1b,
                                               bbcols[:, dc:dc + 1],
                                               xres[un % 2], ALU.add, ALU.add),
                        waits=[("dve", st["bb_done"])], inc="dve")
                    ev_done[un] = S.emit("vector", lambda v, un=un:
                        v.tensor_copy(outbf[un % 2], outst[un % 2]),
                        inc="dve") if dst_bf is not None else S.now("dve")
                    def stf(s, dc=dc, th=th, un=un):
                        sy = sems["sy"]
                        s.dma_start(out=dst[dc * 128:(dc + 1) * 128,
                                            th * 1024:(th + 1) * 1024],
                                    in_=outst[un % 2]).then_inc(sy, 16)
                        if dst_bf is not None:
                            s.dma_start(out=dst_bf[dc * 128:(dc + 1) * 128,
                                                   th * 1024:(th + 1) * 1024],
                                        in_=outbf[un % 2]).then_inc(sy, 16)
                    out_st[un] = S.emit("sync", stf,
                        waits=[("dve", ev_done[un])], inc="sy",
                        amt=32 if dst_bf is not None else 16, self_inc=True)
            st["src_ready_sy"] = S.now("sy")
            st["ar_free_stats"] = S.now("dve")
            st["ar_ffn_free"] = S.now("dve")
            st["ar_rope_free"] = S.now("dve")
            st["xb_free"] = S.now("pe")

        streams = [xT, sA, sB, sA]
        dsts = [sA, sB, sA, outT]
        bf_srcs = [None, xbA, xbB, xbA]
        bf_dsts = [xbA, xbB, xbA, None]
        if n_sub < 4:
            dsts[n_sub - 1] = outT
            bf_dsts[n_sub - 1] = None
        specs = [
            ("attn", dict(qk_parts=[(wqkv, 0, 16, qS, 0), (wqkv, 4096, 4, kS, 0)],
                          v_part=(wqkv, 5120), wo=wo_a)),
            ("ffn", None),
            ("attn", dict(qk_parts=[(wq_b, 0, 16, qS, 0), (wk_b, 0, 4, kS, 0)],
                          v_part=(wv_b, 0), wo=wo_b)),
            ("ffn", None),
        ]
        for li in range(n_sub):
            kind, cfg = specs[li]
            if kind == "attn":
                emit_attn(streams[li], dsts[li], cfg["qk_parts"],
                          cfg["v_part"], cfg["wo"], bf_srcs[li], bf_dsts[li])
            else:
                emit_ffn(streams[li], dsts[li], bf_srcs[li], bf_dsts[li])

        S.emit("sync", lambda s: s.nop(), waits=[("sy", S.now("sy"))])
        S.emit("gpsimd", lambda g: g.nop(), waits=[("gp", S.now("gp"))])

        block = es.enter_context(nc.Block())

        def replay(name, eng):
            for fn, waits, inc, amt, self_inc in S.ops[name]:
                for (s, v) in waits:
                    eng.wait_ge(sems[s], v)
                inst = fn(eng)
                if self_inc:
                    continue
                if inc is not None:
                    if inst is None:
                        inst = eng.nop()
                    inst.then_inc(sems[inc], amt)

        block.gpsimd(lambda g: replay("gpsimd", g))
        block.tensor(lambda t: replay("tensor", t))
        block.scalar(lambda s: replay("scalar", s))
        block.vector(lambda v: replay("vector", v))
        block.sync(lambda sy: replay("sync", sy))
    return nc


def _deinterleave_pair(w, nheads):
    cols = []
    for c in range(nheads):
        pair, parity = c // 2, c % 2
        for half in (0, 1):
            head = pair * 2 + half
            cols.append(head * 128 + parity + 2 * np.arange(64))
    return np.ascontiguousarray(w[:, np.concatenate(cols)])


def _host_inputs(w_qkv, w_o_a, wq_b, wk_b, wv_b, wo_b, gate_w, base_w, base_b,
                 lora_A, lora_B):
    f32 = lambda a: np.ascontiguousarray(np.asarray(a, dtype=np.float32))
    wqkv = f32(w_qkv).copy()
    wqkv[:, 0:4096] = _deinterleave_pair(wqkv[:, 0:4096], 32)
    wqkv[:, 4096:5120] = _deinterleave_pair(wqkv[:, 4096:5120], 8)
    out = {
        "wqkv": wqkv,
        "wo_a": f32(w_o_a),
        "wq_b": _deinterleave_pair(f32(wq_b), 32),
        "wk_b": _deinterleave_pair(f32(wk_b), 8),
        "wv_b": f32(wv_b),
        "wo_b": f32(wo_b),
        "bwT": np.ascontiguousarray(f32(base_w).T),
        "bb1": f32(base_b).reshape(1, D),
        "gwT": np.ascontiguousarray(f32(gate_w).T),
        "aT": np.ascontiguousarray(f32(lora_A).reshape(128, D).T),
        "bT": np.ascontiguousarray(f32(lora_B).transpose(0, 2, 1).reshape(128, D)),
    }
    inv = 1.0 / (10000.0 ** (np.arange(0, 128, 2, dtype=np.float64) / 128.0))
    ang = np.arange(T, dtype=np.float64)[:, None] * inv[None, :]
    c = np.cos(ang).T.astype(np.float32)
    s = np.sin(ang).T.astype(np.float32)
    out["cdup"] = np.ascontiguousarray(np.vstack([c, c]))
    out["sdup"] = np.ascontiguousarray(np.vstack([s, s]))
    kk, qq = np.meshgrid(np.arange(128), np.arange(128), indexing="ij")
    out["tri"] = (kk <= qq).astype(np.float32)
    out["maskb"] = np.where(kk <= qq, 0.0, -1e30).astype(np.float32)
    out["ident"] = np.eye(128, dtype=np.float32)
    e8 = np.zeros((8, 128), dtype=np.float32)
    for e in range(8):
        e8[e, 16 * e:16 * (e + 1)] = 2.0
    out["e8"] = e8
    return out


def kernel(L0_attn_a_x, L0_attn_a_w_qkv, L0_attn_a_w_o, L0_attn_b_wq,
           L0_attn_b_wk, L0_attn_b_wv, L0_attn_b_wo, L0_ffn_gate_w,
           L0_ffn_base_w, L0_ffn_base_b, L0_ffn_lora_A, L0_ffn_lora_B):
    shared = _host_inputs(L0_attn_a_w_qkv, L0_attn_a_w_o, L0_attn_b_wq,
                          L0_attn_b_wk, L0_attn_b_wv, L0_attn_b_wo,
                          L0_ffn_gate_w, L0_ffn_base_w, L0_ffn_base_b,
                          L0_ffn_lora_A, L0_ffn_lora_B)
    x = np.asarray(L0_attn_a_x, dtype=np.float32)
    in_maps = []
    for b in range(8):
        m = dict(shared)
        m["xT"] = np.ascontiguousarray(x[b].T)
        in_maps.append(m)
    nc = build_program(N_SUB)
    res = run_bass_kernel_spmd(nc, in_maps, core_ids=list(range(8)))
    out = np.stack([np.ascontiguousarray(np.asarray(r["outT"]).T)
                    for r in res.results])
    return out.astype(np.float32)
